# version 02 (final)
# baseline (speedup 1.0000x reference)
import contextlib
import numpy as np
import ml_dtypes
import concourse.bass as bass
import concourse.mybir as mybir
from concourse.bass_utils import run_bass_kernel_spmd

F32 = mybir.dt.float32
BF16 = mybir.dt.bfloat16
I32 = mybir.dt.int32
U8 = mybir.dt.uint8
AF = mybir.ActivationFunctionType
ALU = mybir.AluOpType
AX = mybir.AxisListType

NCORE = 8
B, T, D = 2, 16384, 1024
N = B * T
SH = N // NCORE

_NPDT = {F32: np.float32, BF16: ml_dtypes.bfloat16, I32: np.int32, U8: np.uint8}


class PB:
    COMPUTE = ("tensor", "vector", "scalar", "pool")
    PHYS = {"sync": "sync", "gdma": "gpsimd", "coll": "gpsimd",
            "pool": "gpsimd", "tensor": "tensor", "vector": "vector",
            "scalar": "scalar"}
    UNIT = {"sync": 16, "gdma": 16, "coll": 1, "pool": 1,
            "tensor": 1, "vector": 1, "scalar": 1}

    def __init__(self, nc, drain_threshold=10 ** 9):
        self.nc = nc
        self.ops = []
        self.last_write = {}
        self.last_reads = {}
        self.drain_threshold = drain_threshold

    def add(self, engine, emitfn, reads=(), writes=(), kind="op"):
        assert engine in self.PHYS, engine
        idx = len(self.ops)
        deps = set()
        for k in reads:
            if k in self.last_write:
                deps.add(self.last_write[k])
        for k in writes:
            if k in self.last_write:
                deps.add(self.last_write[k])
            for r in self.last_reads.get(k, ()):
                deps.add(r)
        deps.discard(idx)
        self.ops.append([engine, emitfn, tuple(sorted(deps)), kind])
        for k in reads:
            self.last_reads.setdefault(k, []).append(idx)
        for k in writes:
            self.last_write[k] = idx
            self.last_reads[k] = []
        return idx

    def dma(self, out, in_, engine="sync", extra_reads=(), **kw):
        oap, iap = self._ap(out), self._ap(in_)
        def f(eng):
            return eng.dma_start(out=oap, in_=iap, **kw)
        return self.add(engine, f,
                        reads=[self._key(in_)] + list(extra_reads),
                        writes=[self._key(out)], kind="dma")

    def coll(self, kind, op, ins, outs, groups=None):
        groups = groups or [list(range(NCORE))]
        ii = [self._ap(x) for x in ins]
        oo = [self._ap(x) for x in outs]
        def f(eng):
            return eng.collective_compute(kind, op, replica_groups=groups,
                                          ins=ii, outs=oo)
        return self.add("coll", f,
                        reads=[self._key(x) for x in ins],
                        writes=[self._key(x) for x in outs], kind="coll")

    @staticmethod
    def _ap(x):
        return x if isinstance(x, bass.AP) else x[:]

    @staticmethod
    def _key(x):
        t = x.tensor if isinstance(x, bass.AP) else x
        return getattr(t, "name", None) or id(t)

    def emit(self):
        nc = self.nc
        phys, unit = self.PHYS, self.UNIT
        n = len(self.ops)
        need_inc = [False] * n
        for i, (eng, fn, deps, kind) in enumerate(self.ops):
            if kind in ("dma", "coll"):
                need_inc[i] = True
        for i, (eng, fn, deps, kind) in enumerate(self.ops):
            for d in deps:
                deng, dkind = self.ops[d][0], self.ops[d][3]
                if phys[deng] != phys[eng] or dkind in ("dma", "coll"):
                    need_inc[d] = True
        counts = {e: 0 for e in phys}
        inc_val = [0] * n
        for i, (eng, fn, deps, kind) in enumerate(self.ops):
            if need_inc[i]:
                counts[eng] += 1
            inc_val[i] = counts[eng]
        waits = [dict() for _ in range(n)]
        for i, (eng, fn, deps, kind) in enumerate(self.ops):
            for d in deps:
                deng, dkind = self.ops[d][0], self.ops[d][3]
                if phys[deng] != phys[eng] or dkind in ("dma", "coll"):
                    v = inc_val[d]
                    if waits[i].get(deng, 0) < v:
                        waits[i][deng] = v
            if kind in ("dma", "coll") and inc_val[i] > 1:
                if waits[i].get(eng, 0) < inc_val[i] - 1:
                    waits[i][eng] = inc_val[i] - 1
        drains = [False] * n
        pos = [0] * n
        idx_on = {}
        for i, (eng, fn, deps, kind) in enumerate(self.ops):
            p = phys[eng]
            pos[i] = idx_on.get(p, 0)
            idx_on[p] = pos[i] + 1
        for i, (eng, fn, deps, kind) in enumerate(self.ops):
            if eng not in self.COMPUTE:
                continue
            for d in deps:
                deng, dkind = self.ops[d][0], self.ops[d][3]
                if phys[deng] == phys[eng] and dkind not in ("dma", "coll"):
                    if phys[eng] == "tensor":
                        continue
                    if pos[i] - pos[d] <= self.drain_threshold:
                        drains[i] = True
        final_waits = {e: counts[e] * unit[e] for e in counts if counts[e]}

        es = contextlib.ExitStack()
        sems = {e: es.enter_context(nc.semaphore(f"s_{e}"))
                for e in counts if counts[e]}
        block = es.enter_context(nc.Block())
        per_phys = {}
        for i in range(n):
            per_phys.setdefault(phys[self.ops[i][0]], []).append(i)

        ops = self.ops

        def make_body(op_indices, is_gpsimd):
            def body(section_eng):
                satisfied = {}
                for i in op_indices:
                    eng_name, fn, deps, kind = ops[i]
                    for src in sorted(waits[i]):
                        tgt = waits[i][src] * unit[src]
                        if satisfied.get(src, -1) < tgt:
                            section_eng.wait_ge(sems[src], tgt)
                            satisfied[src] = tgt
                    if drains[i]:
                        section_eng.drain()
                    instr = fn(section_eng)
                    if need_inc[i]:
                        if instr is None:
                            raise RuntimeError(f"op {i} needs inc, emitted None")
                        if eng_name == "tensor":
                            instr.then_inc(sems[eng_name], unit[eng_name])
                        elif eng_name in PB.COMPUTE:
                            section_eng.drain().then_inc(sems[eng_name],
                                                         unit[eng_name])
                        else:
                            instr.then_inc(sems[eng_name], unit[eng_name])
                if is_gpsimd:
                    for src in sorted(final_waits):
                        section_eng.wait_ge(sems[src], final_waits[src])
            return body

        if "sync" in per_phys:
            block.sync(make_body(per_phys["sync"], False))
        if "tensor" in per_phys:
            block.tensor(make_body(per_phys["tensor"], False))
        if "vector" in per_phys:
            block.vector(make_body(per_phys["vector"], False))
        if "scalar" in per_phys:
            block.scalar(make_body(per_phys["scalar"], False))
        block.gpsimd(make_body(per_phys.get("gpsimd", []), True))
        es.close()


def op1(pb, engine, method, out, *ins, reads=(), writes=(), **kw):
    oap = PB._ap(out)
    iaps = [PB._ap(x) if isinstance(x, (bass.AP,)) or hasattr(x, "ap") else x
            for x in ins]
    def f(eng):
        return getattr(eng, method)(oap, *iaps, **kw)
    rd = [PB._key(x) for x in ins if isinstance(x, bass.AP) or hasattr(x, "ap")]
    return pb.add(engine, f, reads=list(rd) + list(reads),
                  writes=[PB._key(out)] + list(writes))


def mm_acc(pb, out_psum, pairs, extra_reads=()):
    pairs = [(PB._ap(a), PB._ap(b)) for a, b in pairs]
    oap = PB._ap(out_psum)
    def f(eng):
        last = None
        nn = len(pairs)
        for i, (a, b) in enumerate(pairs):
            last = eng.matmul(oap, a, b, start=(i == 0), stop=(i == nn - 1))
        return last
    reads = [PB._key(a) for a, b in pairs] + [PB._key(b) for a, b in pairs]
    return pb.add("tensor", f, reads=reads + list(extra_reads),
                  writes=[PB._key(out_psum)])


def transpose_op(pb, out_psum, in_ap, ident_ap):
    oap, iap, idap = PB._ap(out_psum), PB._ap(in_ap), PB._ap(ident_ap)
    def f(eng):
        return eng.transpose(oap, iap, idap)
    return pb.add("tensor", f, reads=[PB._key(in_ap), PB._key(ident_ap)],
                  writes=[PB._key(out_psum)])


def act(pb, out, in_, func, bias=0.0, scale=1.0, engine="scalar", accum_out=None):
    oap, iap = PB._ap(out), PB._ap(in_)
    b = PB._ap(bias) if isinstance(bias, bass.AP) or hasattr(bias, "ap") else bias
    s = PB._ap(scale) if isinstance(scale, bass.AP) or hasattr(scale, "ap") else scale
    kw = {}
    if accum_out is not None:
        kw["accum_out"] = PB._ap(accum_out)
    def f(eng):
        return eng.activation(oap, iap, func, bias=b, scale=s, **kw)
    reads = [PB._key(in_)]
    writes = [PB._key(out)]
    for x in (bias, scale):
        if isinstance(x, bass.AP) or hasattr(x, "ap"):
            reads.append(PB._key(x))
    if accum_out is not None:
        writes.append(PB._key(accum_out))
    return pb.add(engine, f, reads=reads, writes=writes)


class ConstPool:
    def __init__(self, pb, es, nc, name="cpool", n=16):
        self.pb = pb
        self.tile = alloc_sb(es, nc, name, [128, n], F32)
        self.cols = {}
        self.n = n

    def col(self, v):
        v = float(v)
        if v not in self.cols:
            i = len(self.cols)
            assert i < self.n
            self.cols[v] = i
            op1(self.pb, "vector", "memset", self.tile[:, i:i + 1], v)
        return self.tile[:, self.cols[v]:self.cols[v] + 1]


def alloc_sb(es, nc, name, shape, dt):
    return es.enter_context(nc.sbuf_tensor(name, list(shape), dt))


def alloc_ps(es, nc, name, shape, dt=F32):
    return es.enter_context(nc.psum_tensor(name, list(shape), dt))


def layernorm_tile(pb, es, nc, xn_out, x_in, stats, eps=1e-5):
    st6 = stats[:, 0:6]
    st2 = stats[:, 6:8]
    op1(pb, "vector", "bn_stats", st6, x_in)
    op1(pb, "vector", "bn_aggr", st2, st6)
    rstd = stats[:, 5:6]
    act(pb, rstd, stats[:, 7:8], AF.Ln, bias=float(eps))
    act(pb, rstd, rstd, AF.Exp, scale=-0.5)
    def f(eng):
        return eng.tensor_scalar(PB._ap(xn_out), PB._ap(x_in),
                                 stats[:, 6:7], stats[:, 5:6],
                                 ALU.subtract, ALU.mult)
    pb.add("vector", f, reads=[PB._key(x_in), PB._key(stats)],
           writes=[PB._key(xn_out)])


def tt(pb, out, a, b, alu, engine="vector"):
    def f(eng):
        return eng.tensor_tensor(PB._ap(out), PB._ap(a), PB._ap(b), alu)
    return pb.add(engine, f, reads=[PB._key(a), PB._key(b)],
                  writes=[PB._key(out)])


def ts(pb, out, a, s1, s2=None, op0=ALU.mult, op1_=ALU.bypass, engine="vector"):
    aps = [PB._ap(x) if isinstance(x, bass.AP) or hasattr(x, "ap") else x
           for x in (s1, s2)]
    def f(eng):
        if s2 is None and op1_ == ALU.bypass:
            return eng.tensor_single_scalar(PB._ap(out), PB._ap(a), aps[0], op0)
        return eng.tensor_scalar(PB._ap(out), PB._ap(a), aps[0], aps[1], op0, op1_)
    reads = [PB._key(a)] + [PB._key(x) for x in (s1, s2)
                            if isinstance(x, bass.AP) or hasattr(x, "ap")]
    return pb.add(engine, f, reads=reads, writes=[PB._key(out)])


def stt(pb, out, in0, scalar, in1, op0, op1_, engine="vector"):
    sap = PB._ap(scalar) if isinstance(scalar, bass.AP) or hasattr(scalar, "ap") else scalar
    def f(eng):
        return eng.scalar_tensor_tensor(PB._ap(out), PB._ap(in0), sap,
                                        PB._ap(in1), op0, op1_)
    reads = [PB._key(in0), PB._key(in1)]
    if isinstance(scalar, bass.AP) or hasattr(scalar, "ap"):
        reads.append(PB._key(scalar))
    return pb.add(engine, f, reads=reads, writes=[PB._key(out)])


def cp(pb, out, in_, engine="vector"):
    def f(eng):
        return eng.tensor_copy(PB._ap(out), PB._ap(in_))
    return pb.add(engine, f, reads=[PB._key(in_)], writes=[PB._key(out)])


def red(pb, out, in_, alu, absval=False, negate=False, engine="vector"):
    def f(eng):
        return eng.tensor_reduce(PB._ap(out), PB._ap(in_), AX.X, alu,
                                 apply_absolute_value=absval, negate=negate)
    return pb.add(engine, f, reads=[PB._key(in_)], writes=[PB._key(out)])


def layernorm128(pb, xn_out, x_in, sq, st, eps_ap):
    red(pb, st[:, 0:1], x_in, ALU.add)
    act(pb, sq, x_in, AF.Square)
    red(pb, st[:, 1:2], sq, ALU.add)
    ts(pb, st[:, 2:3], st[:, 0:1], 1.0 / 1024.0)
    tt(pb, st[:, 3:4], st[:, 2:3], st[:, 2:3], ALU.mult)
    stt(pb, st[:, 4:5], st[:, 1:2], 1.0 / 1024.0, st[:, 3:4],
        ALU.mult, ALU.subtract)
    act(pb, st[:, 5:6], st[:, 4:5], AF.Ln, bias=eps_ap)
    act(pb, st[:, 5:6], st[:, 5:6], AF.Exp, scale=-0.5)
    ts(pb, xn_out, x_in, st[:, 2:3], st[:, 5:6], ALU.subtract, ALU.mult)


def build_L0a():
    nc = bass.Bass("TRN2", target_bir_lowering=False, debug=False)
    x = nc.dram_tensor("x", [SH, D], F32, kind="ExternalInput").ap()
    wqkT = nc.dram_tensor("wqkT", [D, 3072], F32, kind="ExternalInput").ap()
    codeT = nc.dram_tensor("codeT", [64, 256], F32, kind="ExternalInput").ap()
    ident = nc.dram_tensor("ident", [128, 128], F32, kind="ExternalInput").ap()
    o_xnfm = nc.dram_tensor("xnfm", [8, 128, SH], BF16, kind="ExternalOutput").ap()
    o_v = nc.dram_tensor("v", [SH, D], BF16, kind="ExternalOutput").ap()
    o_qc = nc.dram_tensor("qc", [16, 16, SH], F32, kind="ExternalOutput").ap()
    o_kc = nc.dram_tensor("kc", [16, 16, SH], F32, kind="ExternalOutput").ap()
    o_kmax = nc.dram_tensor("kmax", [16, 16], F32, kind="ExternalOutput").ap()
    o_xnmax = nc.dram_tensor("xnmax", [128, 1], F32, kind="ExternalOutput").ap()

    es = contextlib.ExitStack()
    A = lambda name, shape, dt=F32: alloc_sb(es, nc, name, shape, dt)
    P = lambda name, shape, dt=F32: alloc_ps(es, nc, name, shape, dt)
    wqk = A("wqk", [128, 8 * 3072], BF16)
    code = A("code", [64, 256], BF16)
    idb = A("idb", [128, 128], BF16)
    xt = [A(f"xt{i}", [128, 4 * 1024], F32) for i in range(2)]
    xnt = [A(f"xnt{i}", [128, 4 * 1024], BF16) for i in range(2)]
    xnfm = [A(f"xnfm{i}", [128, 8 * 512], BF16) for i in range(2)]
    vt = [A(f"vt{i}", [128, 4 * 1024], BF16) for i in range(2)]
    qch = [A(f"qch{i}", [16, 512], F32) for i in range(2)]
    hA = A("hA", [64, 512], BF16)
    hB = A("hB", [64, 512], BF16)
    sq = A("sq", [128, 1024], F32)
    st = A("st", [128, 8], F32)
    kmax_acc = A("kmax_acc", [16, 16], F32)
    kt = A("kt", [16, 1], F32)
    xnmax_acc = A("xnmax_acc", [128, 2], F32)
    ps_tr = P("ps_tr", [128, 128], BF16)
    ps_v = [P(f"ps_v{i}", [128, 512]) for i in range(2)]
    ps_qk = [P(f"ps_qk{i}", [128, 512]) for i in range(2)]
    ps_c = [P(f"ps_c{i}", [16, 512]) for i in range(2)]

    pb = PB(nc)
    cpool = ConstPool(pb, es, nc)
    eps_ap = cpool.col(1e-5)
    for dc in range(8):
        pb.dma(wqk[:, dc * 3072:(dc + 1) * 3072], wqkT[dc * 128:(dc + 1) * 128, :],
               engine="gdma")
    pb.dma(code[:], codeT, engine="gdma")
    pb.dma(idb[:], ident, engine="gdma")
    op1(pb, "vector", "memset", kmax_acc[:], -3.0e38)
    op1(pb, "vector", "memset", xnmax_acc[:], 0.0)

    NSLAB = SH // 512
    for s in range(NSLAB):
        xts, xnts, xf, vts = xt[s % 2], xnt[s % 2], xnfm[s % 2], vt[s % 2]
        for j in range(4):
            pb.dma(xts[:, j * 1024:(j + 1) * 1024],
                   x[s * 512 + j * 128: s * 512 + (j + 1) * 128, :], engine="sync")
        for j in range(4):
            xj = xts[:, j * 1024:(j + 1) * 1024]
            xnj = xnts[:, j * 1024:(j + 1) * 1024]
            layernorm128(pb, xnj, xj, sq, st, eps_ap)
            red(pb, xnmax_acc[:, 1:2], xnj, ALU.max, absval=True)
            tt(pb, xnmax_acc[:, 0:1], xnmax_acc[:, 0:1], xnmax_acc[:, 1:2], ALU.max)
            for dc in range(8):
                transpose_op(pb, ps_tr, xnj[:, dc * 128:(dc + 1) * 128], idb)
                cp(pb, xf[:, dc * 512 + j * 128: dc * 512 + j * 128 + 128], ps_tr)
        for j in range(4):
            for half in range(2):
                pv = ps_v[half]
                mm_acc(pb, pv, [
                    (xf[:, dc * 512 + j * 128: dc * 512 + j * 128 + 128],
                     wqk[:, dc * 3072 + 2048 + half * 512: dc * 3072 + 2048 + (half + 1) * 512])
                    for dc in range(8)])
                cp(pb, vts[:, j * 1024 + half * 512: j * 1024 + (half + 1) * 512], pv)
        for j in range(4):
            pb.dma(o_v[s * 512 + j * 128: s * 512 + (j + 1) * 128, :],
                   vts[:, j * 1024:(j + 1) * 1024], engine="sync")
        for dc in range(8):
            pb.dma(o_xnfm[dc, :, s * 512:(s + 1) * 512], xf[:, dc * 512:(dc + 1) * 512],
                   engine="sync")
        for fb in range(16):
            pq = ps_qk[fb % 2]
            mm_acc(pb, pq, [
                (wqk[:, dc * 3072 + fb * 128:dc * 3072 + (fb + 1) * 128],
                 xf[:, dc * 512:(dc + 1) * 512]) for dc in range(8)])
            cp(pb, hA[:], pq[0:64, :])
            cp(pb, hB[:], pq[64:128, :])
            for hh, hbuf in ((2 * (fb % 8), hA), (2 * (fb % 8) + 1, hB)):
                pc = ps_c[hh % 2]
                is_q = fb < 8
                mm_acc(pb, pc, [(code[:, hh * 16:(hh + 1) * 16], hbuf)])
                qc_s = qch[hh % 2]
                ts(pb, qc_s[:], pc, 0.125)
                dst = (o_qc if is_q else o_kc)
                pb.dma(dst[hh, :, s * 512:(s + 1) * 512], qc_s[:], engine="sync")
                if not is_q:
                    red(pb, kt[:], qc_s[:], ALU.max)
                    tt(pb, kmax_acc[:, hh:hh + 1], kmax_acc[:, hh:hh + 1], kt[:], ALU.max)
    pb.dma(o_kmax, kmax_acc[:], engine="sync")
    pb.dma(o_xnmax, xnmax_acc[:, 0:1], engine="sync")
    pb.emit()
    es.close()
    return nc


def part_reduce_via_dram(pb, es, nc, out_row, acc_col, scratch_name, alu=ALU.max):
    dscr = nc.dram_tensor(scratch_name, [128], F32).ap()
    row = alloc_sb(es, nc, scratch_name + "_row", [1, 128], F32)
    pb.dma(dscr, acc_col, engine="sync")
    pb.dma(row[:], dscr.rearrange("(one p) -> one p", one=1), engine="sync")
    red(pb, out_row, row[:], alu)


def bcast_row(pb, out_sb, row_ap, ones_col_bf, ps):
    mm_acc(pb, ps, [(ones_col_bf, row_ap)])
    cp(pb, out_sb, ps)


def build_L0b():
    nc = bass.Bass("TRN2", target_bir_lowering=False, debug=False)
    kc = nc.dram_tensor("kc", [16, 16, SH], F32, kind="ExternalInput").ap()
    v = nc.dram_tensor("v", [SH, D], BF16, kind="ExternalInput").ap()
    kmax_all = nc.dram_tensor("kmax_all", [4, 16, 16], F32, kind="ExternalInput").ap()
    ident = nc.dram_tensor("ident", [128, 128], F32, kind="ExternalInput").ap()
    o_kmax = nc.dram_tensor("kmax_g", [16, 16], F32, kind="ExternalOutput").ap()
    o_tot = nc.dram_tensor("totals", [16, 16, 65], F32, kind="ExternalOutput").ap()

    es = contextlib.ExitStack()
    A = lambda name, shape, dt=F32: alloc_sb(es, nc, name, shape, dt)
    P = lambda name, shape, dt=F32: alloc_ps(es, nc, name, shape, dt)
    kml = A("kml", [16, 16 * 4], F32)
    kmg = A("kmg", [16, 16], F32)
    kmn = A("kmn", [16, 16], F32)
    kcb = A("kcb", [16, SH], F32)
    keb = A("keb", [16, SH], BF16)
    ket = A("ket", [128, 16 * 32], BF16)
    vtb = [A(f"vtb{i}", [128, 64 * 32], BF16) for i in range(2)]
    idb = A("idb", [128, 128], BF16)
    tot = A("tot", [16, 65], F32)
    ps_tr = P("ps_tr", [128, 16], BF16)
    ps_kv = P("ps_kv", [16, 64])

    pb = PB(nc)
    pb.dma(idb[:], ident, engine="gdma")
    for p in range(4):
        pb.dma(kml[:, p * 16:(p + 1) * 16], kmax_all[p], engine="sync")
    tt(pb, kmg[:], kml[:, 0:16], kml[:, 16:32], ALU.max)
    tt(pb, kmg[:], kmg[:], kml[:, 32:48], ALU.max)
    tt(pb, kmg[:], kmg[:], kml[:, 48:64], ALU.max)
    ts(pb, kmn[:], kmg[:], -1.0)
    pb.dma(o_kmax, kmg[:], engine="sync")

    for h in range(16):
        vh = vtb[h % 2]
        pb.dma(vh[:].rearrange("p (j dv) -> p j dv", j=32),
               v[:, h * 64:(h + 1) * 64].rearrange("(j p) dv -> p j dv", p=128),
               engine="sync")
        pb.dma(kcb[:], kc[h], engine="sync")
        act(pb, keb[:], kcb[:], AF.Exp, bias=kmn[:, h:h + 1])
        red(pb, tot[:, 0:1], keb[:], ALU.add)
        for j in range(32):
            transpose_op(pb, ps_tr, keb[:, j * 128:(j + 1) * 128], idb[0:16, 0:16])
            cp(pb, ket[:, j * 16:(j + 1) * 16], ps_tr)
        mm_acc(pb, ps_kv, [
            (ket[:, j * 16:(j + 1) * 16], vh[:, j * 64:(j + 1) * 64])
            for j in range(32)])
        cp(pb, tot[:, 1:65], ps_kv)
        pb.dma(o_tot[h], tot[:], engine="sync")
    pb.emit()
    es.close()
    return nc


def build_L0c():
    nc = bass.Bass("TRN2", target_bir_lowering=False, debug=False)
    qc = nc.dram_tensor("qc", [16, 16, SH], F32, kind="ExternalInput").ap()
    kc = nc.dram_tensor("kc", [16, 16, SH], F32, kind="ExternalInput").ap()
    v = nc.dram_tensor("v", [SH, D], BF16, kind="ExternalInput").ap()
    kmax_g = nc.dram_tensor("kmax_g", [16, 16], F32, kind="ExternalInput").ap()
    ptot = nc.dram_tensor("ptot", [3, 16, 16, 65], F32, kind="ExternalInput").ap()
    woT = nc.dram_tensor("woT", [D, D], F32, kind="ExternalInput").ap()
    ident = nc.dram_tensor("ident", [128, 128], F32, kind="ExternalInput").ap()
    triu = nc.dram_tensor("triu", [128, 128], F32, kind="ExternalInput").ap()
    o_s = nc.dram_tensor("s0", [SH, D], F32, kind="ExternalOutput").ap()

    TS = 2048
    NSL = SH // TS
    NCH = TS // 128

    es = contextlib.ExitStack()
    A = lambda name, shape, dt=F32: alloc_sb(es, nc, name, shape, dt)
    P = lambda name, shape, dt=F32: alloc_ps(es, nc, name, shape, dt)
    wo = A("wo", [128, 8 * D], BF16)
    idb = A("idb", [128, 128], BF16)
    trb = A("trb", [128, 128], BF16)
    kmn = A("kmn", [16, 16], F32)
    pt = A("pt", [16, 3 * 16 * 65], F32)
    ksi = A("ksi", [16, 16], F32)
    ones16 = A("ones16", [16, TS], F32)
    onec = A("onec", [16, 1], F32)
    one1 = A("one1", [1, 16], F32)
    kcb = A("kcb", [16, TS], F32)
    kec = A("kec", [16, TS], F32)
    kecb = A("kecb", [16, TS], BF16)
    ket = A("ket", [128, 16 * NCH], BF16)
    qcb = A("qcb", [16, TS], F32)
    qs = A("qs", [16, TS], F32)
    qden = A("qden", [1, TS], F32)
    qdb = A("qdb", [16, TS], F32)
    kscum = A("kscum", [16, TS], F32)
    ksc = A("ksc", [16, 2], F32)
    qfb = A("qfb", [16, TS], BF16)
    vtb = A("vtb", [128, 64 * NCH], BF16)
    kvf = A("kvf", [16, 64], F32)
    kvb = A("kvb", [16, 64], BF16)
    stp = A("stp", [128, 128], BF16)
    xo_all = A("xo_all", [128, 32 * D], BF16)
    xofm = A("xofm", [128, 8 * 128], BF16)
    s0t = A("s0t", [128, D], F32)
    ps_p = P("ps_p", [128, 128])
    ps_xo = P("ps_xo", [128, 64])
    ps_kv = P("ps_kv", [16, 64])
    ps_q1 = P("ps_q1", [1, 512])
    ps_qb = P("ps_qb", [16, 512])
    ps_tr = P("ps_tr", [128, 128], BF16)
    ps_o = [P(f"ps_o{i}", [128, 512]) for i in range(2)]

    pb = PB(nc)
    for dc in range(8):
        pb.dma(wo[:, dc * D:(dc + 1) * D], woT[dc * 128:(dc + 1) * 128, :],
               engine="gdma")
    pb.dma(idb[:], ident, engine="gdma")
    pb.dma(trb[:], triu, engine="gdma")
    pb.dma(kmn[:], kmax_g, engine="sync")
    ts(pb, kmn[:], kmn[:], -1.0)
    pb.dma(pt[:].rearrange("c (p h e) -> c p h e", p=3, h=16),
           ptot.transpose([2, 0, 1, 3]), engine="sync")
    pt4 = pt[:].rearrange("c (p h e) -> c p h e", p=3, h=16)
    tt(pb, ksi[:], pt4[:, 0, :, 0], pt4[:, 1, :, 0], ALU.add)
    tt(pb, ksi[:], ksi[:], pt4[:, 2, :, 0], ALU.add)
    op1(pb, "vector", "memset", ones16[:], 1.0)
    op1(pb, "vector", "memset", onec[:], 1.0)
    op1(pb, "vector", "memset", one1[:], 1.0)

    for h in range(16):
        cp(pb, ksc[:, 0:1], ksi[:, h:h + 1])
        tt(pb, kvf[:], pt4[:, 0, h, 1:65], pt4[:, 1, h, 1:65], ALU.add)
        tt(pb, kvf[:], kvf[:], pt4[:, 2, h, 1:65], ALU.add)
        cp(pb, kvb[:], kvf[:])
        for sl in range(NSL):
            t0 = sl * TS
            pb.dma(vtb[:].rearrange("p (j dv) -> p j dv", j=NCH),
                   v[t0:t0 + TS, h * 64:(h + 1) * 64].rearrange(
                       "(j p) dv -> p j dv", p=128), engine="sync")
            pb.dma(kcb[:], kc[h, :, t0:t0 + TS], engine="sync")
            act(pb, kec[:], kcb[:], AF.Exp, bias=kmn[:, h:h + 1])
            cp(pb, kecb[:], kec[:])
            for j in range(NCH):
                transpose_op(pb, ps_tr[:, 0:16], kecb[:, j * 128:(j + 1) * 128],
                             idb[0:16, 0:16])
                cp(pb, ket[:, j * 16:(j + 1) * 16], ps_tr[0:128, 0:16])
            def scanf(eng, _o=kscum[:], _a=ones16[:], _b=kec[:]):
                return eng.tensor_tensor_scan(_o, _a, _b, 0.0, ALU.mult, ALU.add)
            pb.add("vector", scanf, reads=[PB._key(ones16), PB._key(kec)],
                   writes=[PB._key(kscum)])
            ts(pb, kscum[:], kscum[:], ksc[:, 0:1], None, ALU.add)
            cp(pb, ksc[:, 0:1], kscum[:, TS - 1:TS])
            pb.dma(qcb[:], qc[h, :, t0:t0 + TS], engine="sync")
            act(pb, qs[:], qcb[:], AF.Exp)
            for j in range(TS // 512):
                s5 = slice(j * 512, (j + 1) * 512)
                mm_acc(pb, ps_q1, [(onec[:], qs[:, s5])])
                cp(pb, qden[:, s5], ps_q1)
            op1(pb, "vector", "reciprocal", qden[:], qden[:])
            for j in range(TS // 512):
                s5 = slice(j * 512, (j + 1) * 512)
                mm_acc(pb, ps_qb, [(one1[:], qden[:, s5])])
                cp(pb, qdb[:, s5], ps_qb)
            ts(pb, kscum[:], kscum[:], 1e-9, None, ALU.add)
            op1(pb, "vector", "reciprocal", kscum[:], kscum[:])
            tt(pb, qs[:], qs[:], qdb[:], ALU.mult)
            tt(pb, qs[:], qs[:], kscum[:], ALU.mult)
            ts(pb, qfb[:], qs[:], 0.125)
            for j in range(NCH):
                cj = slice(j * 128, (j + 1) * 128)
                jj = sl * NCH + j
                mm_acc(pb, ps_p, [(kecb[:, cj], qfb[:, cj])])
                tt(pb, stp[:], ps_p, trb[:], ALU.mult)
                def xomm(eng, _ps=ps_xo[:], _q=qfb[:, cj], _kv=kvb[:],
                         _s=stp[:], _v=vtb[:, j * 64:(j + 1) * 64]):
                    eng.matmul(_ps, _q, _kv, start=True, stop=False)
                    return eng.matmul(_ps, _s, _v, start=False, stop=True)
                pb.add("tensor", xomm,
                       reads=[PB._key(qfb), PB._key(kvb), PB._key(stp),
                              PB._key(vtb)],
                       writes=[PB._key(ps_xo)])
                cp(pb, xo_all[:, jj * D + h * 64: jj * D + (h + 1) * 64], ps_xo)
                mm_acc(pb, ps_kv, [(ket[:, j * 16:(j + 1) * 16],
                                    vtb[:, j * 64:(j + 1) * 64])])
                tt(pb, kvf[:], kvf[:], ps_kv, ALU.add)
                cp(pb, kvb[:], kvf[:])
    for j in range(32):
        for dc in range(8):
            transpose_op(pb, ps_tr,
                         xo_all[:, j * D + dc * 128:j * D + (dc + 1) * 128], idb)
            cp(pb, xofm[:, dc * 128:(dc + 1) * 128], ps_tr)
        for half in range(2):
            po = ps_o[half]
            mm_acc(pb, po, [
                (xofm[:, dc * 128:(dc + 1) * 128],
                 wo[:, dc * D + half * 512: dc * D + (half + 1) * 512])
                for dc in range(8)])
            cp(pb, s0t[:, half * 512:(half + 1) * 512], po)
        pb.dma(o_s[j * 128:(j + 1) * 128, :], s0t[:], engine="sync")
    pb.emit()
    es.close()
    return nc


def repl_scalar(pb, out_col, src11, ones_1x128, ps128):
    mm_acc(pb, ps128, [(ones_1x128, src11)])
    cp(pb, out_col, ps128)


def quant_round(pb, out_bf, in_f32, inv_ap, si32, back, dbuf, adj):
    act(pb, si32, in_f32, AF.Copy, scale=inv_ap)
    act(pb, back, si32, AF.Copy)
    stt(pb, dbuf, in_f32, inv_ap, back, ALU.mult, ALU.subtract)
    ts(pb, adj, dbuf, 0.5, None, ALU.is_gt)
    tt(pb, back, back, adj, ALU.add)
    ts(pb, adj, dbuf, -0.5, None, ALU.is_lt)
    tt(pb, out_bf, back, adj, ALU.subtract)


def build_L0f():
    nc = bass.Bass("TRN2", target_bir_lowering=False, debug=False)
    xnfm = nc.dram_tensor("xnfm", [8, 128, SH], BF16, kind="ExternalInput").ap()
    xnmax_all = nc.dram_tensor("xnmax_all", [1, 1024], F32, kind="ExternalInput").ap()
    w1T = nc.dram_tensor("w1T", [D, 4096], F32, kind="ExternalInput").ap()
    b1f = nc.dram_tensor("b1f", [128, 32], F32, kind="ExternalInput").ap()
    w2T = nc.dram_tensor("w2T", [4096, D], F32, kind="ExternalInput").ap()
    b2r = nc.dram_tensor("b2r", [1, D], F32, kind="ExternalInput").ap()
    s0 = nc.dram_tensor("s0", [SH, D], F32, kind="ExternalInput").ap()
    x = nc.dram_tensor("x", [SH, D], F32, kind="ExternalInput").ap()
    o_x1 = nc.dram_tensor("x1", [SH, D], F32, kind="ExternalOutput").ap()
    h_fm = nc.dram_tensor("h_fm", [32, 128, SH], BF16)
    cbar = nc.dram_tensor("cbar", [1, 4], F32)
    cbar_o = nc.dram_tensor("cbar_o", [1, 4], F32, addr_space="Shared")

    es = contextlib.ExitStack()
    A = lambda name, shape, dt=F32: alloc_sb(es, nc, name, shape, dt)
    P = lambda name, shape, dt=F32: alloc_ps(es, nc, name, shape, dt)
    w1q = A("w1q", [128, 8 * 4096], BF16)
    w2q = A("w2q", [128, 32 * D], BF16)
    wst = A("wst", [128, 1024], F32)
    si32 = A("si32", [128, 1024], I32)
    back = A("back", [128, 1024], F32)
    dbuf = A("dbuf", [128, 1024], F32)
    xq = [A(f"xq{i}", [128, 8 * 256], BF16) for i in range(2)]
    hb = [A(f"hb{i}", [128, 32 * 256], BF16) for i in range(2)]
    hq = hb
    xs0 = A("xs0", [128, D], F32)
    xxt = A("xxt", [128, D], F32)
    gte = A("gte", [128, D], F32)
    sc = A("sc", [128, 16], F32)
    row = A("row", [1, 1024], F32)
    r11 = A("r11", [1, 8], F32)
    ones_r = A("ones_r", [1, 128], F32)
    b1s = A("b1s", [128, 32], F32)
    b2bc = A("b2bc", [128, D], F32)
    macc = A("macc", [128, 2], F32)
    ps_a = [P(f"ps_a{i}", [128, 512]) for i in range(2)]
    ps_c = P("ps_c", [128, 1])

    pb = PB(nc)
    op1(pb, "vector", "memset", ones_r[:], 1.0)
    pb.dma(row[:], xnmax_all, engine="sync")
    red(pb, r11[:, 0:1], row[:], ALU.max)
    ts(pb, r11[:, 0:1], r11[:, 0:1], 1e-6, None, ALU.max)
    ts(pb, r11[:, 1:2], r11[:, 0:1], 1.0 / 127.0)
    op1(pb, "vector", "reciprocal", r11[:, 2:3], r11[:, 1:2])
    repl_scalar(pb, sc[:, 0:1], r11[:, 2:3], ones_r, ps_c)
    op1(pb, "vector", "memset", macc[:], 0.0)
    for ch in range(8):
        for st8 in range(4):
            sl = slice(st8 * 1024, (st8 + 1) * 1024)
            pb.dma(wst[:], w1T[ch * 128:(ch + 1) * 128, sl], engine="sync")
            red(pb, macc[:, 1:2], wst[:], ALU.max, absval=True)
            tt(pb, macc[:, 0:1], macc[:, 0:1], macc[:, 1:2], ALU.max)
    part_reduce_via_dram(pb, es, nc, r11[:, 3:4], macc[:, 0:1], "w1red")
    ts(pb, r11[:, 3:4], r11[:, 3:4], 1e-6, None, ALU.max)
    ts(pb, r11[:, 4:5], r11[:, 3:4], 1.0 / 127.0)
    op1(pb, "vector", "reciprocal", r11[:, 5:6], r11[:, 4:5])
    repl_scalar(pb, sc[:, 1:2], r11[:, 5:6], ones_r, ps_c)
    for ch in range(8):
        for st8 in range(4):
            sl = slice(st8 * 1024, (st8 + 1) * 1024)
            pb.dma(wst[:], w1T[ch * 128:(ch + 1) * 128, sl], engine="sync")
            quant_round(pb, w1q[:, ch * 4096 + st8 * 1024: ch * 4096 + (st8 + 1) * 1024],
                        wst[:], sc[:, 1:2], si32, back, dbuf, wst)
    op1(pb, "vector", "memset", macc[:], 0.0)
    for ch in range(32):
        sl2 = slice(ch * 128, (ch + 1) * 128)
        pb.dma(wst[:], w2T[sl2, :], engine="sync")
        red(pb, macc[:, 1:2], wst[:], ALU.max, absval=True)
        tt(pb, macc[:, 0:1], macc[:, 0:1], macc[:, 1:2], ALU.max)
    part_reduce_via_dram(pb, es, nc, r11[:, 6:7], macc[:, 0:1], "w2red")
    ts(pb, r11[:, 6:7], r11[:, 6:7], 1e-6, None, ALU.max)
    ts(pb, r11[:, 7:8], r11[:, 6:7], 1.0 / 127.0)
    op1(pb, "vector", "reciprocal", r11[:, 0:1], r11[:, 7:8])
    repl_scalar(pb, sc[:, 2:3], r11[:, 0:1], ones_r, ps_c)
    for ch in range(32):
        sl2 = slice(ch * 128, (ch + 1) * 128)
        pb.dma(wst[:], w2T[sl2, :], engine="sync")
        quant_round(pb, w2q[:, ch * D:(ch + 1) * D], wst[:],
                    sc[:, 2:3], si32, back, dbuf, wst)
    tt(pb, r11[:, 1:2], r11[:, 1:2], r11[:, 4:5], ALU.mult)
    repl_scalar(pb, sc[:, 3:4], r11[:, 1:2], ones_r, ps_c)
    pb.dma(b1s[:], b1f, engine="sync")
    ts(pb, b1s[:], b1s[:], sc[:, 3:4], None, ALU.mult)
    op1(pb, "vector", "memset", macc[:], 0.0)
    NSLAB = SH // 256
    for s in range(NSLAB):
        xqs, hbs = xq[s % 2], hb[s % 2]
        pb.dma(xqs[:].rearrange("p (c t) -> p c t", c=8),
               xnfm[:, :, s * 256:(s + 1) * 256].transpose([1, 0, 2]),
               engine="sync")
        for c8 in range(2):
            sl = slice(c8 * 1024, (c8 + 1) * 1024)
            quant_round(pb, xqs[:, sl], xqs[:, sl], sc[:, 0:1],
                        si32, back, dbuf, wst)
        for fb in range(32):
            pa = ps_a[fb % 2]
            mm_acc(pb, pa[:, 0:256], [
                (w1q[:, dc * 4096 + fb * 128: dc * 4096 + (fb + 1) * 128],
                 xqs[:, dc * 256:(dc + 1) * 256]) for dc in range(8)])
            act(pb, hbs[:, fb * 256:(fb + 1) * 256], pa[:, 0:256], AF.Gelu,
                bias=b1s[:, fb:fb + 1], scale=sc[:, 3:4])
            red(pb, macc[:, 1:2], hbs[:, fb * 256:(fb + 1) * 256],
                ALU.max, absval=True)
            tt(pb, macc[:, 0:1], macc[:, 0:1], macc[:, 1:2], ALU.max)
        pb.dma(h_fm.transpose([1, 0, 2])[:, :, s * 256:(s + 1) * 256],
               hbs[:].rearrange("p (c t) -> p c t", c=32), engine="sync")
    part_reduce_via_dram(pb, es, nc, r11[:, 1:2], macc[:, 0:1], "hred")
    cp(pb, row[0:1, 0:4], r11[:, 1:2].broadcast_to([1, 4]))
    pb.dma(cbar.ap(), row[0:1, 0:4], engine="gdma")
    pb.coll("AllReduce", ALU.max, [cbar.ap()], [cbar_o.ap()])
    pb.dma(r11[:, 1:2], cbar_o.ap()[0:1, 0:1], engine="gdma")
    ts(pb, r11[:, 1:2], r11[:, 1:2], 1e-6, None, ALU.max)
    ts(pb, r11[:, 2:3], r11[:, 1:2], 1.0 / 127.0)
    op1(pb, "vector", "reciprocal", r11[:, 3:4], r11[:, 2:3])
    repl_scalar(pb, sc[:, 4:5], r11[:, 3:4], ones_r, ps_c)
    tt(pb, r11[:, 2:3], r11[:, 2:3], r11[:, 7:8], ALU.mult)
    repl_scalar(pb, sc[:, 5:6], r11[:, 2:3], ones_r, ps_c)
    pb.dma(row[:], b2r, engine="sync")
    for half in range(2):
        mm_acc(pb, ps_a[0], [(ones_r, row[0:1, half * 512:(half + 1) * 512])])
        cp(pb, b2bc[:, half * 512:(half + 1) * 512], ps_a[0])
    ts(pb, b2bc[:], b2bc[:], sc[:, 5:6], None, ALU.mult)
    for s in range(NSLAB):
        hqs = hq[s % 2]
        pb.dma(hqs[:].rearrange("p (c t) -> p c t", c=32),
               h_fm.transpose([1, 0, 2])[:, :, s * 256:(s + 1) * 256],
               engine="sync")
        for st8 in range(8):
            sl = slice(st8 * 1024, (st8 + 1) * 1024)
            quant_round(pb, hqs[:, sl], hqs[:, sl], sc[:, 4:5],
                        si32, back, dbuf, wst)
        for tb in range(2):
            tok = slice(s * 256 + tb * 128, s * 256 + (tb + 1) * 128)
            pb.dma(xs0[:], s0[tok, :], engine="sync")
            pb.dma(xxt[:], x[tok, :], engine="sync")
            for half in range(2):
                pa = ps_a[half]
                mm_acc(pb, pa, [
                    (hqs[:, fb * 256 + tb * 128: fb * 256 + tb * 128 + 128],
                     w2q[:, fb * D + half * 512: fb * D + (half + 1) * 512])
                    for fb in range(32)])
                stt(pb, gte[:, half * 512:(half + 1) * 512], pa, sc[:, 5:6],
                    b2bc[:, half * 512:(half + 1) * 512], ALU.mult, ALU.add)
            act(pb, gte[:], gte[:], AF.Silu)
            tt(pb, gte[:], xs0[:], gte[:], ALU.mult)
            tt(pb, gte[:], gte[:], xxt[:], ALU.add)
            pb.dma(o_x1[tok, :], gte[:], engine="sync")
    pb.emit()
    es.close()
    return nc


def build_L1a():
    nc = bass.Bass("TRN2", target_bir_lowering=False, debug=False)
    x = nc.dram_tensor("x", [SH, D], F32, kind="ExternalInput").ap()
    inpw = nc.dram_tensor("inpw", [D, 4096], F32, kind="ExternalInput").ap()
    sim = nc.dram_tensor("sim", [D, 16], F32, kind="ExternalInput").ap()
    gates = nc.dram_tensor("gates", [1, 16], F32, kind="ExternalInput").ap()
    ident = nc.dram_tensor("ident", [128, 128], F32, kind="ExternalInput").ap()
    o_xin = nc.dram_tensor("xin", [16, 128, SH], BF16, kind="ExternalOutput").ap()
    o_z = nc.dram_tensor("z", [16, 128, SH], BF16, kind="ExternalOutput").ap()
    o_w = nc.dram_tensor("w_rt", [SH, 16], F32, kind="ExternalOutput").ap()
    o_wr = nc.dram_tensor("w_row", [1, 8 * 8192], F32, kind="ExternalOutput").ap()
    o_xnfm = nc.dram_tensor("xnfm", [8, 128, SH], BF16, kind="ExternalOutput").ap()

    es = contextlib.ExitStack()
    A = lambda name, shape, dt=F32: alloc_sb(es, nc, name, shape, dt)
    P = lambda name, shape, dt=F32: alloc_ps(es, nc, name, shape, dt)
    wip = A("wip", [128, 8 * 4096], BF16)
    idb = A("idb", [128, 128], BF16)
    idf = A("idf", [128, 128], F32)
    simn = A("simn", [128, 8 * 16], BF16)
    sgb = A("sgb", [128, 16], F32)
    snb = A("snb", [128, 16], F32)
    snrm = A("snrm", [1, 16], F32)
    xt = [A(f"xt{i}", [128, 4 * 1024], F32) for i in range(2)]
    xnt = [A(f"xnt{i}", [128, 4 * 1024], BF16) for i in range(2)]
    xf = [A(f"xf{i}", [128, 8 * 512], BF16) for i in range(2)]
    xzc = A("xzc", [128, 512], BF16)
    sq = A("sq", [128, 1024], F32)
    st = A("st", [128, 8], F32)
    nrm4 = A("nrm4", [128, 8], F32)
    lgt = A("lgt", [128, 16], F32)
    rt1 = A("rt1", [128, 16], F32)
    rt2 = A("rt2", [128, 16], F32)
    rt3 = A("rt3", [128, 16], F32)
    wrt = A("wrt", [128, 16], F32)
    rcol = A("rcol", [128, 8], F32)
    wcm = A("wcm", [16, 512], F32)
    wrow = A("wrow", [1, 8192], F32)
    ones_r = A("ones_r", [1, 128], F32)
    ones_c = A("ones_c", [128, 1], F32)
    cst = A("cst", [128, 2], F32)
    wcm_dram = nc.dram_tensor("wcm_scr", [16, 512], F32)
    ps_tr = P("ps_tr", [128, 128], BF16)
    ps_tw = P("ps_tw", [16, 128])
    ps_m = [P(f"ps_m{i}", [128, 512]) for i in range(2)]
    ps_l = P("ps_l", [128, 16])
    ps_s = P("ps_s", [1, 16])

    pb = PB(nc)
    op1(pb, "vector", "memset", ones_r[:], 1.0)
    op1(pb, "vector", "memset", ones_c[:], 1.0)
    op1(pb, "vector", "memset", cst[:, 0:1], 1e-5)
    for dc in range(8):
        pb.dma(wip[:, dc * 4096:(dc + 1) * 4096], inpw[dc * 128:(dc + 1) * 128, :],
               engine="gdma")
    pb.dma(idb[:], ident, engine="gdma")
    pb.dma(idf[:], ident, engine="sync")
    pb.dma(simn[:].rearrange("p (c e) -> p c e", c=8),
           sim.rearrange("(c p) e -> p c e", p=128), engine="gdma")
    for c8 in range(8):
        pb.dma(sq[:, 0:16], sim[c8 * 128:(c8 + 1) * 128, :], engine="sync")
        act(pb, sq[:, 16:32], sq[:, 0:16], AF.Square)
        def f(eng, _o=ps_s[:], _a=ones_c[:], _b=sq[:, 16:32],
              _st=(c8 == 0), _sp=(c8 == 7)):
            return eng.matmul(_o, _a, _b, start=_st, stop=_sp)
        pb.add("tensor", f, reads=["ones_c", "sq"], writes=["ps_s"])
    cp(pb, snrm[:], ps_s)
    act(pb, snrm[:], snrm[:], AF.Sqrt)
    ts(pb, snrm[:], snrm[:], 1e-12, None, ALU.max)
    op1(pb, "vector", "reciprocal", snrm[:], snrm[:])
    mm_acc(pb, ps_l, [(ones_r, snrm[:])])
    cp(pb, snb[:], ps_l)
    for c8 in range(8):
        tt(pb, simn[:, c8 * 16:(c8 + 1) * 16], simn[:, c8 * 16:(c8 + 1) * 16],
           snb[:], ALU.mult)
    pb.dma(snrm[:], gates, engine="sync")
    act(pb, snrm[:], snrm[:], AF.Sigmoid)
    mm_acc(pb, ps_l, [(ones_r, snrm[:])])
    cp(pb, sgb[:], ps_l)

    NSLAB = SH // 512
    for s in range(NSLAB):
        xts, xnts, xfs = xt[s % 2], xnt[s % 2], xf[s % 2]
        for j in range(4):
            pb.dma(xts[:, j * 1024:(j + 1) * 1024],
                   x[s * 512 + j * 128: s * 512 + (j + 1) * 128, :], engine="sync")
        for j in range(4):
            xj = xts[:, j * 1024:(j + 1) * 1024]
            xnj = xnts[:, j * 1024:(j + 1) * 1024]
            layernorm128(pb, xnj, xj, sq, st, cst[:, 0:1])
            for dc in range(8):
                transpose_op(pb, ps_tr, xnj[:, dc * 128:(dc + 1) * 128], idb)
                cp(pb, xfs[:, dc * 512 + j * 128: dc * 512 + j * 128 + 128], ps_tr)
            act(pb, sq, xnj, AF.Square)
            red(pb, nrm4[:, j:j + 1], sq, ALU.add)
            act(pb, nrm4[:, j:j + 1], nrm4[:, j:j + 1], AF.Sqrt)
            ts(pb, nrm4[:, j:j + 1], nrm4[:, j:j + 1], 1e-12, None, ALU.max)
            op1(pb, "vector", "reciprocal", nrm4[:, 4 + j:5 + j], nrm4[:, j:j + 1])
        for fb in range(32):
            pm = ps_m[fb % 2]
            mm_acc(pb, pm, [
                (wip[:, dc * 4096 + fb * 128: dc * 4096 + (fb + 1) * 128],
                 xfs[:, dc * 512:(dc + 1) * 512]) for dc in range(8)])
            cp(pb, xzc[:], pm)
            dst = o_xin if fb < 16 else o_z
            fb2 = fb if fb < 16 else fb - 16
            pb.dma(dst[fb2, :, s * 512:(s + 1) * 512], xzc[:], engine="sync")
        for j in range(4):
            mm_acc(pb, ps_l, [
                (xfs[:, dc * 512 + j * 128: dc * 512 + j * 128 + 128],
                 simn[:, dc * 16:(dc + 1) * 16]) for dc in range(8)])
            stt(pb, lgt[:], ps_l, nrm4[:, 4 + j:5 + j], sgb[:], ALU.mult,
                ALU.subtract)
            ts(pb, rt1[:], lgt[:], 0.0, None, ALU.is_gt)
            ts(pb, rt2[:], lgt[:], 0.0, None, ALU.max)
            red(pb, rcol[:, 0:1], rt2[:], ALU.max)
            ts(pb, rcol[:, 1:2], rcol[:, 0:1], -1.0)
            act(pb, rt3[:], rt2[:], AF.Exp, bias=rcol[:, 1:2])
            tt(pb, rt3[:], rt3[:], rt1[:], ALU.mult)
            red(pb, rcol[:, 2:3], rt3[:], ALU.add)
            ts(pb, rcol[:, 2:3], rcol[:, 2:3], 1e-30, None, ALU.add)
            op1(pb, "vector", "reciprocal", rcol[:, 3:4], rcol[:, 2:3])
            ts(pb, rt3[:], rt3[:], rcol[:, 3:4], None, ALU.mult)
            red(pb, rcol[:, 4:5], rt1[:], ALU.add)
            ts(pb, rcol[:, 4:5], rcol[:, 4:5], 0.5, None, ALU.is_gt)
            red(pb, rcol[:, 5:6], lgt[:], ALU.max)
            ts(pb, rt2[:], lgt[:], rcol[:, 5:6], None, ALU.is_ge)
            stt(pb, rt1[:], rt2[:], -1e38, lgt[:], ALU.mult, ALU.add)
            red(pb, rcol[:, 6:7], rt1[:], ALU.max)
            ts(pb, rt1[:], rt1[:], rcol[:, 6:7], None, ALU.is_ge)
            tt(pb, rt2[:], rt2[:], rt1[:], ALU.add)
            ts(pb, rt2[:], rt2[:], 0.5)
            tt(pb, rt3[:], rt3[:], rt2[:], ALU.subtract)
            stt(pb, wrt[:], rt3[:], rcol[:, 4:5], rt2[:], ALU.mult, ALU.add)
            pb.dma(o_w[s * 512 + j * 128: s * 512 + (j + 1) * 128, :], wrt[:],
                   engine="sync")
            transpose_op(pb, ps_tw, wrt[:], idf)
            cp(pb, wcm[:, j * 128:(j + 1) * 128], ps_tw)
        pb.dma(wcm_dram.ap(), wcm[:], engine="sync")
        pb.dma(wrow[:], wcm_dram.ap().rearrange("e t -> (e t)").rearrange(
            "(one f) -> one f", one=1), engine="sync")
        pb.dma(o_wr[0:1, s * 8192:(s + 1) * 8192], wrow[:], engine="sync")
        for dc in range(8):
            pb.dma(o_xnfm[dc, :, s * 512:(s + 1) * 512],
                   xfs[:, dc * 512:(dc + 1) * 512], engine="sync")
    pb.emit()
    es.close()
    return nc


def build_L1a2():
    nc = bass.Bass("TRN2", target_bir_lowering=False, debug=False)
    xnfm = nc.dram_tensor("xnfm", [8, 128, SH], BF16, kind="ExternalInput").ap()
    wr = nc.dram_tensor("w_row", [1, 8 * 8192], F32, kind="ExternalInput").ap()
    qp = nc.dram_tensor("qp", [16, D, 128], F32, kind="ExternalInput").ap()
    kp = nc.dram_tensor("kp", [16, D, 128], F32, kind="ExternalInput").ap()
    vp = nc.dram_tensor("vp", [16, D, 128], F32, kind="ExternalInput").ap()
    ident = nc.dram_tensor("ident", [128, 128], F32, kind="ExternalInput").ap()
    o_q = nc.dram_tensor("q", [128, SH], BF16, kind="ExternalOutput").ap()
    o_k = nc.dram_tensor("k", [128, SH], BF16, kind="ExternalOutput").ap()
    o_v = nc.dram_tensor("v", [SH, 128], BF16, kind="ExternalOutput").ap()

    es = contextlib.ExitStack()
    A = lambda name, shape, dt=F32: alloc_sb(es, nc, name, shape, dt)
    P = lambda name, shape, dt=F32: alloc_ps(es, nc, name, shape, dt)
    wqkv = A("wqkv", [128, 3 * 16 * 8 * 128], BF16)
    idb = A("idb", [128, 128], BF16)
    xfs2 = [A(f"xfs2{i}", [128, 8 * 512], BF16) for i in range(2)]
    wrow = A("wrow", [1, 8192], F32)
    wbc = A("wbc", [128, 512], F32)
    qacc = A("qacc", [128, 512], F32)
    kacc = A("kacc", [128, 512], F32)
    vacc = A("vacc", [128, 512], F32)
    tmp5 = A("tmp5", [128, 512], F32)
    accb = A("accb", [128, 512], BF16)
    vtt = A("vtt", [128, 4 * 128], BF16)
    ones_r = A("ones_r", [1, 128], F32)
    ps_m = [P(f"ps_m{i}", [128, 512]) for i in range(2)]
    ps_b = P("ps_b", [128, 512])
    ps_tr = P("ps_tr", [128, 128], BF16)

    pb = PB(nc)
    op1(pb, "vector", "memset", ones_r[:], 1.0)
    pb.dma(idb[:], ident, engine="gdma")
    for pi, proj in enumerate((qp, kp, vp)):
        for e in range(16):
            for dc in range(8):
                off = pi * 16 * 8 * 128 + e * 8 * 128 + dc * 128
                pb.dma(wqkv[:, off:off + 128],
                       proj[e, dc * 128:(dc + 1) * 128, :], engine="gdma")
    NSLAB = SH // 512
    for s in range(NSLAB):
        xfs = xfs2[s % 2]
        pb.dma(xfs[:].rearrange("p (c t) -> p c t", c=8),
               xnfm[:, :, s * 512:(s + 1) * 512].transpose([1, 0, 2]),
               engine="sync")
        pb.dma(wrow[:], wr[0:1, s * 8192:(s + 1) * 8192], engine="sync")
        for e in range(16):
            mm_acc(pb, ps_b, [(ones_r, wrow[0:1, e * 512:(e + 1) * 512])])
            cp(pb, wbc[:], ps_b)
            for pi in range(3):
                pm = ps_m[(e * 3 + pi) % 2]
                off = pi * 16 * 8 * 128 + e * 8 * 128
                mm_acc(pb, pm, [
                    (wqkv[:, off + dc * 128: off + (dc + 1) * 128],
                     xfs[:, dc * 512:(dc + 1) * 512]) for dc in range(8)])
                acc = (qacc, kacc, vacc)[pi]
                if e == 0:
                    tt(pb, acc[:], pm, wbc[:], ALU.mult)
                else:
                    tt(pb, tmp5[:], pm, wbc[:], ALU.mult)
                    tt(pb, acc[:], acc[:], tmp5[:], ALU.add)
        pb.dma(o_q[:, s * 512:(s + 1) * 512], qacc[:], engine="gdma")
        pb.dma(o_k[:, s * 512:(s + 1) * 512], kacc[:], engine="gdma")
        cp(pb, accb[:], vacc[:])
        for j4 in range(4):
            transpose_op(pb, ps_tr, accb[:, j4 * 128:(j4 + 1) * 128], idb)
            cp(pb, vtt[:, j4 * 128:(j4 + 1) * 128], ps_tr)
            pb.dma(o_v[s * 512 + j4 * 128: s * 512 + (j4 + 1) * 128, :],
                   vtt[:, j4 * 128:(j4 + 1) * 128], engine="sync")
    pb.emit()
    es.close()
    return nc


def build_L1b():
    nc = bass.Bass("TRN2", target_bir_lowering=False, debug=False)
    xin = nc.dram_tensor("xin", [16, 128, SH], BF16, kind="ExternalInput").ap()
    halo = nc.dram_tensor("halo", [16, 128, 3], F32, kind="ExternalInput").ap()
    convw = nc.dram_tensor("convw", [16, 128, 4], F32, kind="ExternalInput").ap()
    convb = nc.dram_tensor("convb", [16, 128, 1], F32, kind="ExternalInput").ap()
    xpw = nc.dram_tensor("xpw", [2048, 33], F32, kind="ExternalInput").ap()
    o_xc = nc.dram_tensor("xc", [16, 128, SH], BF16, kind="ExternalOutput").ap()
    o_ssm = nc.dram_tensor("ssm", [SH, 33], F32, kind="ExternalOutput").ap()

    es = contextlib.ExitStack()
    A = lambda name, shape, dt=F32: alloc_sb(es, nc, name, shape, dt)
    P = lambda name, shape, dt=F32: alloc_ps(es, nc, name, shape, dt)
    xinb = A("xinb", [128, SH + 3], BF16)
    accv = A("accv", [128, SH], F32)
    xcb = A("xcb", [128, SH], BF16)
    cw = A("cw", [128, 16 * 5], F32)
    xpb = A("xpb", [128, 16 * 33], BF16)
    xcl = A("xcl", [128, 16 * 128], BF16)
    sso = A("sso", [128, 33], F32)
    ps_s = P("ps_s", [128, 33])
    pb = PB(nc)
    for ch in range(16):
        pb.dma(cw[:, ch * 5:ch * 5 + 4], convw[ch], engine="sync")
        pb.dma(cw[:, ch * 5 + 4:ch * 5 + 5], convb[ch], engine="sync")
        pb.dma(xpb[:, ch * 33:(ch + 1) * 33], xpw[ch * 128:(ch + 1) * 128, :],
               engine="gdma")
    for ch in range(16):
        pb.dma(xinb[:, 0:3], halo[ch], engine="gdma")
        pb.dma(xinb[:, 3:SH + 3], xin[ch], engine="sync")
        ts(pb, accv[:], xinb[:, 0:SH], cw[:, ch * 5:ch * 5 + 1],
           cw[:, ch * 5 + 4:ch * 5 + 5], ALU.mult, ALU.add)
        for j in range(1, 4):
            stt(pb, accv[:], xinb[:, j:SH + j], cw[:, ch * 5 + j:ch * 5 + j + 1],
                accv[:], ALU.mult, ALU.add)
        act(pb, xcb[:], accv[:], AF.Silu)
        pb.dma(o_xc[ch], xcb[:], engine="sync")
    for tb in range(SH // 128):
        pb.dma(xcl[:].rearrange("p (c t) -> p c t", c=16),
               o_xc[:, :, tb * 128:(tb + 1) * 128].transpose([1, 0, 2]),
               engine="sync")
        mm_acc(pb, ps_s, [
            (xcl[:, ch * 128:(ch + 1) * 128], xpb[:, ch * 33:(ch + 1) * 33])
            for ch in range(16)])
        cp(pb, sso[:], ps_s)
        pb.dma(o_ssm[tb * 128:(tb + 1) * 128, :], sso[:], engine="sync")
    pb.emit()
    es.close()
    return nc


def build_L1s():
    TB = T
    nc = bass.Bass("TRN2", target_bir_lowering=False, debug=False)
    xcu = nc.dram_tensor("xcu", [4, TB, 128], BF16, kind="ExternalInput").ap()
    ssml = nc.dram_tensor("ssml", [4, TB], F32, kind="ExternalInput").ap()
    dtw = nc.dram_tensor("dtw", [4, 1, 128], F32, kind="ExternalInput").ap()
    dtb = nc.dram_tensor("dtb", [4, 1, 128], F32, kind="ExternalInput").ap()
    alog = nc.dram_tensor("alog", [4, 128, 16], F32, kind="ExternalInput").ap()
    dpar = nc.dram_tensor("dpar", [4, 1, 128], F32, kind="ExternalInput").ap()
    Bp = nc.dram_tensor("Bp", [4, TB, 16], F32, kind="ExternalInput").ap()
    Cp = nc.dram_tensor("Cp", [4, TB, 16], F32, kind="ExternalInput").ap()
    triu32 = nc.dram_tensor("triu32", [128, 128], F32, kind="ExternalInput").ap()
    sel4 = nc.dram_tensor("sel4", [128, 4], F32, kind="ExternalInput").ap()
    ind4 = nc.dram_tensor("ind4", [4, 128], F32, kind="ExternalInput").ap()
    o_y = nc.dram_tensor("y", [4, TB, 128], BF16, kind="ExternalOutput").ap()
    hscr = nc.dram_tensor("hscr", [4, 2048], F32)
    hscr2 = nc.dram_tensor("hscr2", [4, 128], F32)

    es = contextlib.ExitStack()
    A = lambda name, shape, dt=F32: alloc_sb(es, nc, name, shape, dt)
    P = lambda name, shape, dt=F32: alloc_ps(es, nc, name, shape, dt)
    tri = A("tri", [128, 128], BF16)
    self_ = A("self_", [128, 4], BF16)
    selff = A("selff", [128, 4], F32)
    indbb = A("indbb", [4, 128], BF16)
    d2b = A("d2b", [128, 128], BF16)
    h4b = A("h4b", [4, 2048], BF16)
    indb = A("indb", [4, 128], F32)
    ones_r = A("ones_r", [1, 128], F32)
    arep = A("arep", [128, 2048], F32)
    dtwb = A("dtwb", [128, 128], F32)
    dtbb = A("dtbb", [128, 128], F32)
    dbb = A("dbb", [128, 128], F32)
    arow = A("arow", [128, 16], F32)
    xct = [A(f"xct{i}", [128, 128], BF16) for i in range(2)]
    Bt = [A(f"Bt{i}", [128, 16], F32) for i in range(2)]
    Ct = [A(f"Ct{i}", [128, 16], F32) for i in range(2)]
    slt = [A(f"slt{i}", [128, 1], F32) for i in range(2)]
    dele = A("dele", [128, 128], F32)
    d2 = A("d2", [128, 128], F32)
    d3 = A("d3", [128, 128], F32)
    cd = A("cd", [128, 128], F32)
    cumb = A("cumb", [128, 2048], F32)
    cda = A("cda", [128, 2048], BF16)
    invb = A("invb", [128, 2048], BF16)
    wfull = A("wfull", [128, 2048], BF16)
    cs_sb = A("cs_sb", [128, 2048], BF16)
    states = A("states", [128, 2048], BF16)
    yv = A("yv", [128, 128], F32)
    yb = A("yb", [128, 128], BF16)
    hrow = A("hrow", [1, 2048], F32)
    hcs = A("hcs", [1, 8192], F32)
    hcda = A("hcda", [1, 8192], F32)
    hinj = A("hinj", [1, 8192], F32)
    h4 = A("h4", [4, 2048], F32)
    cdl = A("cdl", [4, 128], F32)
    cdl1 = A("cdl1", [1, 128 * 4], F32)
    ps_cd = P("ps_cd", [128, 128])
    ps_cs = [P(f"ps_cs{i}", [128, 512]) for i in range(4)]
    ps_x1 = P("ps_x1", [4, 512])
    ps_x2 = P("ps_x2", [4, 128])
    pb = PB(nc)
    pb.dma(tri[:], triu32, engine="gdma")
    pb.dma(self_[:], sel4, engine="gdma")
    pb.dma(selff[:], sel4, engine="sync")
    pb.dma(indb[:], ind4, engine="sync")
    pb.dma(indbb[:], ind4, engine="gdma")
    op1(pb, "vector", "memset", ones_r[:], 1.0)

    for u in range(4):
        pb.dma(arow[:], alog[u], engine="sync")
        act(pb, arow[:], arow[:], AF.Exp)
        ts(pb, arow[:], arow[:], -1.0)
        pb.dma(hscr.ap()[0:1, :].rearrange("one f -> (one f)"),
               arow[:].rearrange("c s -> c s"), engine="sync")
        pb.dma(hrow[:], hscr.ap()[0:1, :], engine="sync")
        for nh in range(4):
            def f(eng, _o=ps_cs[nh][:], _a=ones_r[:],
                  _b=hrow[0:1, nh * 512:(nh + 1) * 512]):
                return eng.matmul(_o, _a, _b, start=True, stop=True)
            pb.add("tensor", f, reads=["ones_r", "hrow"], writes=[f"ps_cs{nh}"])
            cp(pb, arep[:, nh * 512:(nh + 1) * 512], ps_cs[nh])
        for (dst, srcd) in ((dtwb, dtw), (dtbb, dtb), (dbb, dpar)):
            pb.dma(hrow[0:1, 0:128], srcd[u], engine="sync")
            def f(eng, _o=ps_cd[:], _a=ones_r[:], _b=hrow[0:1, 0:128]):
                return eng.matmul(_o, _a, _b, start=True, stop=True)
            pb.add("tensor", f, reads=["ones_r", "hrow"], writes=["ps_cd"])
            cp(pb, dst[:], ps_cd)
        op1(pb, "vector", "memset", hrow[:], 0.0)
        for tb in range(TB // 128):
            t0 = tb * 128
            xc_t, Btt, Ctt, sl_t = (xct[tb % 2], Bt[tb % 2], Ct[tb % 2],
                                    slt[tb % 2])
            pb.dma(xc_t[:], xcu[u, t0:t0 + 128, :], engine="sync")
            pb.dma(Btt[:], Bp[u, t0:t0 + 128, :], engine="sync")
            pb.dma(Ctt[:], Cp[u, t0:t0 + 128, :], engine="sync")
            pb.dma(sl_t[:], ssml[u:u + 1, t0:t0 + 128].rearrange(
                "one t -> (one t)").rearrange("(t one) -> t one", one=1),
                engine="sync")
            ts(pb, dele[:], dtwb[:], sl_t[:], None, ALU.mult)
            tt(pb, dele[:], dele[:], dtbb[:], ALU.add)
            ts(pb, d2[:], dele[:], 0.0, None, ALU.max)
            act(pb, d3[:], dele[:], AF.Abs)
            act(pb, d3[:], d3[:], AF.Exp, scale=-1.0)
            ts(pb, d3[:], d3[:], 1.0, None, ALU.add)
            act(pb, d3[:], d3[:], AF.Ln)
            tt(pb, dele[:], d2[:], d3[:], ALU.add)
            cp(pb, d2b[:], dele[:])
            def f(eng, _o=ps_cd[:], _a=tri[:], _b=d2b[:]):
                return eng.matmul(_o, _a, _b, start=True, stop=True)
            pb.add("tensor", f, reads=["tri", "d2b"], writes=["ps_cd"])
            cp(pb, cd[:], ps_cd)
            tt(pb, cumb[:].rearrange("p (c s) -> p c s", s=16), arep[:].rearrange(
                "p (c s) -> p c s", s=16),
               cd[:].unsqueeze(2).broadcast_to([128, 128, 16]), ALU.mult)
            act(pb, cda[:], cumb[:], AF.Exp)
            ts(pb, cumb[:], cumb[:], -20.0, -1.0, ALU.max, ALU.mult)
            act(pb, invb[:], cumb[:], AF.Exp)
            tt(pb, d3[:], dele[:], xc_t[:], ALU.mult)
            tt(pb, wfull[:].rearrange("p (c s) -> p c s", s=16),
               invb[:].rearrange("p (c s) -> p c s", s=16),
               Btt[:].unsqueeze(1).broadcast_to([128, 128, 16]), ALU.mult)
            tt(pb, wfull[:].rearrange("p (c s) -> p c s", s=16),
               wfull[:].rearrange("p (c s) -> p c s", s=16),
               d3[:].unsqueeze(2).broadcast_to([128, 128, 16]), ALU.mult)
            for nh in range(4):
                def f(eng, _o=ps_cs[nh][:], _a=tri[:],
                      _b=wfull[:, nh * 512:(nh + 1) * 512]):
                    return eng.matmul(_o, _a, _b, start=True, stop=False)
                pb.add("tensor", f, reads=["tri", "wfull"], writes=[f"ps_cs{nh}"])
                cp(pb, cs_sb[:, nh * 512:(nh + 1) * 512], ps_cs[nh])
            for nh in range(4):
                def f(eng, _o=ps_x1[:], _a=self_[:],
                      _b=cs_sb[:, nh * 512:(nh + 1) * 512]):
                    return eng.matmul(_o, _a, _b, start=True, stop=True)
                pb.add("tensor", f, reads=["self_", "cs_sb"], writes=["ps_x1"])
                cp(pb, h4[:, nh * 512:(nh + 1) * 512], ps_x1)
            def f(eng, _o=ps_x2[:], _a=selff[:], _b=cd[:]):
                return eng.matmul(_o, _a, _b, start=True, stop=True)
            pb.add("tensor", f, reads=["selff", "cd"], writes=["ps_x2"])
            cp(pb, cdl[:], ps_x2)
            pb.dma(hscr.ap(), h4[:], engine="sync")
            pb.dma(hcs[:], hscr.ap().rearrange("a b -> (a b)").rearrange(
                "(one f) -> one f", one=1), engine="sync")
            pb.dma(hscr2.ap(), cdl[:], engine="sync")
            pb.dma(cdl1[:], hscr2.ap().rearrange("a b -> (a b)").rearrange(
                "(one f) -> one f", one=1), engine="sync")
            for r in range(4):
                tt(pb, hcda[0:1, r * 2048:(r + 1) * 2048].rearrange(
                    "one (c s) -> one c s", s=16),
                   arep[0:1, :].rearrange("one (c s) -> one c s", s=16),
                   cdl1[0:1, r * 128:(r + 1) * 128].unsqueeze(2).broadcast_to(
                       [1, 128, 16]), ALU.mult)
            act(pb, hcda[:], hcda[:], AF.Exp)
            for r in range(4):
                cp(pb, hinj[0:1, r * 2048:(r + 1) * 2048], hrow[:])
                tt(pb, hrow[:], hrow[:], hcs[0:1, r * 2048:(r + 1) * 2048],
                   ALU.add)
                tt(pb, hrow[:], hrow[:], hcda[0:1, r * 2048:(r + 1) * 2048],
                   ALU.mult)
            pb.dma(hscr.ap().rearrange("a b -> (a b)").rearrange(
                "(one f) -> one f", one=1), hinj[:], engine="sync")
            pb.dma(h4[:], hscr.ap(), engine="sync")
            cp(pb, h4b[:], h4[:])
            hb4 = h4b
            for nh in range(4):
                def f(eng, _o=ps_cs[nh][:], _a=indbb[:],
                      _b=hb4[:, nh * 512:(nh + 1) * 512]):
                    return eng.matmul(_o, _a, _b, start=False, stop=True,
                                      skip_group_check=True)
                pb.add("tensor", f, reads=["indbb", "h4b"], writes=[f"ps_cs{nh}"])
                tt(pb, states[:, nh * 512:(nh + 1) * 512],
                   cda[:, nh * 512:(nh + 1) * 512], ps_cs[nh], ALU.mult)
            tt(pb, cumb[:].rearrange("p (c s) -> p c s", s=16),
               states[:].rearrange("p (c s) -> p c s", s=16),
               Ctt[:].unsqueeze(1).broadcast_to([128, 128, 16]), ALU.mult)
            red(pb, yv[:], cumb[:].rearrange("p (c s) -> p c s", s=16), ALU.add)
            tt(pb, d3[:], xc_t[:], dbb[:], ALU.mult)
            tt(pb, yv[:], yv[:], d3[:], ALU.add)
            cp(pb, yb[:], yv[:])
            pb.dma(o_y[u, t0:t0 + 128, :], yb[:], engine="sync")
    pb.emit()
    es.close()
    return nc


def build_L1fb():
    nc = bass.Bass("TRN2", target_bir_lowering=False, debug=False)
    q = nc.dram_tensor("q", [128, SH], BF16, kind="ExternalInput").ap()
    kf = nc.dram_tensor("kf", [128, T], BF16, kind="ExternalInput").ap()
    vf = nc.dram_tensor("vf", [T, 128], BF16, kind="ExternalInput").ap()
    w_rtT = nc.dram_tensor("w_rtT", [16, SH], F32, kind="ExternalInput").ap()
    op_ = nc.dram_tensor("oproj", [16, 128, D], F32, kind="ExternalInput").ap()
    o_g = nc.dram_tensor("g1", [SH, D], F32, kind="ExternalOutput").ap()

    SCL = 1.0 / float(np.sqrt(128.0))
    es = contextlib.ExitStack()
    A = lambda name, shape, dt=F32: alloc_sb(es, nc, name, shape, dt)
    P = lambda name, shape, dt=F32: alloc_ps(es, nc, name, shape, dt)
    opw = A("opw", [128, 16 * D], BF16)
    kt = A("kt", [128, T], BF16)
    qt = A("qt", [128, 512], BF16)
    vt = [A(f"vt{i}", [128, 128], BF16) for i in range(2)]
    mxall = A("mxall", [128, 8], F32)
    stp = A("stp", [128, 512], BF16)
    nmb = A("nmb", [128, 512], F32)
    lrow = A("lrow", [1, 512], F32)
    wl = A("wl", [1, 512], F32)
    wlb = A("wlb", [128, 512], F32)
    oeh = A("oeh", [128, 16 * 512], BF16)
    oth = A("oth", [128, 512], F32)
    gt_ = A("gt_", [128, D], F32)
    ones_r = A("ones_r", [1, 128], F32)
    ones_cb = A("ones_cb", [128, 1], BF16)
    mrow_d = nc.dram_tensor("mrow_scr", [128, 4], F32)
    ps_s = [P(f"ps_s{i}", [128, 512]) for i in range(2)]
    ps_st = P("ps_st", [128, 512])
    ps_ot = P("ps_ot", [128, 512])
    ps_l = P("ps_l", [1, 512])
    ps_b = P("ps_b", [128, 512])
    ps_o = [P(f"ps_o{i}", [128, 512]) for i in range(2)]

    pb = PB(nc)
    op1(pb, "vector", "memset", ones_r[:], 1.0)
    op1(pb, "vector", "memset", ones_cb[:], 1.0)
    for e in range(16):
        pb.dma(opw[:, e * D:(e + 1) * D], op_[e], engine="gdma")
    pb.dma(kt[:], kf, engine="sync")
    NQS = SH // 512
    for qi in range(NQS):
        pb.dma(qt[:], q[:, qi * 512:(qi + 1) * 512], engine="sync")
        for sub in range(4):
            op1(pb, "vector", "memset", mxall[:, sub:sub + 1], -3.0e38)
        for sub in range(4):
            qsl = qt[:, sub * 128:(sub + 1) * 128]
            for ks in range(T // 512):
                pp = ps_s[ks % 2]
                mm_acc(pb, pp, [(qsl, kt[:, ks * 512:(ks + 1) * 512])])
                red(pb, mxall[:, 4 + sub:5 + sub], pp, ALU.max)
                tt(pb, mxall[:, sub:sub + 1], mxall[:, sub:sub + 1],
                   mxall[:, 4 + sub:5 + sub], ALU.max)
        ts(pb, mxall[:, 0:4], mxall[:, 0:4], SCL)
        pb.dma(mrow_d.ap(), mxall[:, 0:4], engine="sync")
        pb.dma(wl[:].rearrange("one (s p) -> one s p", s=4),
               mrow_d.ap().transpose([1, 0]).unsqueeze(0), engine="sync",
               allow_slow_non_contiguous=True)
        mm_acc(pb, ps_b, [(ones_r, wl[:])])
        cp(pb, nmb[:], ps_b)
        for kt128 in range(T // 128):
            vts = vt[kt128 % 2]
            pb.dma(vts[:], vf[kt128 * 128:(kt128 + 1) * 128, :], engine="sync")
            mm_acc(pb, ps_st, [(kt[:, kt128 * 128:(kt128 + 1) * 128], qt[:])])
            stt(pb, stp[:], ps_st, SCL, nmb[:], ALU.mult, ALU.subtract)
            act(pb, stp[:], stp[:], AF.Exp)
            first = (kt128 == 0)
            last = (kt128 == T // 128 - 1)
            def f1(eng, _o=ps_ot[:], _a=vts[:], _b=stp[:], _st=first, _sp=last):
                return eng.matmul(_o, _a, _b, start=_st, stop=_sp,
                                  skip_group_check=True)
            pb.add("tensor", f1, reads=["vt0", "vt1", "stp"], writes=["ps_ot"])
            def f2(eng, _o=ps_l[:], _a=ones_cb[:], _b=stp[:], _st=first, _sp=last):
                return eng.matmul(_o, _a, _b, start=_st, stop=_sp,
                                  skip_group_check=True)
            pb.add("tensor", f2, reads=["ones_cb", "stp"], writes=["ps_l"])
        cp(pb, oth[:], ps_ot)
        cp(pb, lrow[:], ps_l)
        op1(pb, "vector", "reciprocal", lrow[:], lrow[:])
        for e in range(16):
            pb.dma(wl[:], w_rtT[e:e + 1, qi * 512:(qi + 1) * 512], engine="sync")
            tt(pb, wl[:], wl[:], lrow[:], ALU.mult)
            mm_acc(pb, ps_b, [(ones_r, wl[:])])
            cp(pb, wlb[:], ps_b)
            tt(pb, oeh[:, e * 512:(e + 1) * 512], oth[:], wlb[:], ALU.mult)
        for sub in range(4):
            for half in range(2):
                ph = ps_o[half]
                mm_acc(pb, ph, [
                    (oeh[:, e * 512 + sub * 128: e * 512 + sub * 128 + 128],
                     opw[:, e * D + half * 512: e * D + (half + 1) * 512])
                    for e in range(16)])
                act(pb, gt_[:, half * 512:(half + 1) * 512], ph, AF.Silu)
            pb.dma(o_g[qi * 512 + sub * 128: qi * 512 + (sub + 1) * 128, :],
                   gt_[:], engine="sync")
    pb.emit()
    es.close()
    return nc


def build_L1d():
    nc = bass.Bass("TRN2", target_bir_lowering=False, debug=False)
    y_cm = nc.dram_tensor("y_cm", [16, 128, SH], BF16, kind="ExternalInput").ap()
    z_cm = nc.dram_tensor("z_cm", [16, 128, SH], BF16, kind="ExternalInput").ap()
    opw = nc.dram_tensor("opw", [2048, D], F32, kind="ExternalInput").ap()
    g1 = nc.dram_tensor("g1", [SH, D], F32, kind="ExternalInput").ap()
    x1 = nc.dram_tensor("x1", [SH, D], F32, kind="ExternalInput").ap()
    o_x2 = nc.dram_tensor("x2", [SH, D], F32, kind="ExternalOutput").ap()

    es = contextlib.ExitStack()
    A = lambda name, shape, dt=F32: alloc_sb(es, nc, name, shape, dt)
    P = lambda name, shape, dt=F32: alloc_ps(es, nc, name, shape, dt)
    ow = A("ow", [128, 16 * D], BF16)
    yz = [A(f"yz{i}", [128, 16 * 128], BF16) for i in range(2)]
    zt = A("zt", [128, 16 * 128], BF16)
    s1t = A("s1t", [128, D], F32)
    g1t = A("g1t", [128, D], F32)
    x1t = A("x1t", [128, D], F32)
    ps_o = [P(f"ps_o{i}", [128, 512]) for i in range(2)]
    pb = PB(nc)
    for ch in range(16):
        pb.dma(ow[:, ch * D:(ch + 1) * D], opw[ch * 128:(ch + 1) * 128, :],
               engine="gdma")
    for tb in range(SH // 128):
        yzs = yz[tb % 2]
        pb.dma(yzs[:].rearrange("p (c t) -> p c t", c=16),
               y_cm[:, :, tb * 128:(tb + 1) * 128].transpose([1, 0, 2]),
               engine="sync")
        pb.dma(zt[:].rearrange("p (c t) -> p c t", c=16),
               z_cm[:, :, tb * 128:(tb + 1) * 128].transpose([1, 0, 2]),
               engine="sync")
        act(pb, zt[:], zt[:], AF.Silu)
        tt(pb, yzs[:], yzs[:], zt[:], ALU.mult)
        for half in range(2):
            mm_acc(pb, ps_o[half], [
                (yzs[:, ch * 128:(ch + 1) * 128],
                 ow[:, ch * D + half * 512: ch * D + (half + 1) * 512])
                for ch in range(16)])
            cp(pb, s1t[:, half * 512:(half + 1) * 512], ps_o[half])
        pb.dma(g1t[:], g1[tb * 128:(tb + 1) * 128, :], engine="sync")
        pb.dma(x1t[:], x1[tb * 128:(tb + 1) * 128, :], engine="sync")
        tt(pb, s1t[:], s1t[:], g1t[:], ALU.mult)
        tt(pb, s1t[:], s1t[:], x1t[:], ALU.add)
        pb.dma(o_x2[tb * 128:(tb + 1) * 128, :], s1t[:], engine="sync")
    pb.emit()
    es.close()
    return nc


def rowbc(pb, out_sb, row_ap, ones_r, ps):
    mm_acc(pb, ps, [(ones_r, row_ap)])
    cp(pb, out_sb, ps)


def build_L2a():
    nc = bass.Bass("TRN2", target_bir_lowering=False, debug=False)
    x = nc.dram_tensor("x", [SH, D], F32, kind="ExternalInput").ap()
    xhal = nc.dram_tensor("xhal", [3, D], F32, kind="ExternalInput").ap()
    wnorm = nc.dram_tensor("wnorm", [1, D], F32, kind="ExternalInput").ap()
    cw4 = nc.dram_tensor("cw4", [8, 128, 4], F32, kind="ExternalInput").ap()
    cb4 = nc.dram_tensor("cb4", [8, 128, 1], F32, kind="ExternalInput").ap()
    wq = nc.dram_tensor("wq", [D, 512], F32, kind="ExternalInput").ap()
    wk = nc.dram_tensor("wk", [D, 512], F32, kind="ExternalInput").ap()
    wv = nc.dram_tensor("wv", [D, D], F32, kind="ExternalInput").ap()
    wsp = nc.dram_tensor("wsp", [D, 512], F32, kind="ExternalInput").ap()
    wau = nc.dram_tensor("wau", [D, 64], F32, kind="ExternalInput").ap()
    bau = nc.dram_tensor("bau", [1, 64], F32, kind="ExternalInput").ap()
    wad = nc.dram_tensor("wad", [64, 512], F32, kind="ExternalInput").ap()
    bad = nc.dram_tensor("bad", [1, 512], F32, kind="ExternalInput").ap()
    wbet = nc.dram_tensor("wbet", [D, 8], F32, kind="ExternalInput").ap()
    bbet = nc.dram_tensor("bbet", [1, 8], F32, kind="ExternalInput").ap()
    wu1 = nc.dram_tensor("wu1", [D, 64], F32, kind="ExternalInput").ap()
    bu1 = nc.dram_tensor("bu1", [1, 64], F32, kind="ExternalInput").ap()
    wu2 = nc.dram_tensor("wu2", [64, D], F32, kind="ExternalInput").ap()
    bu2 = nc.dram_tensor("bu2", [1, D], F32, kind="ExternalInput").ap()
    ident = nc.dram_tensor("ident", [128, 128], F32, kind="ExternalInput").ap()
    o_xnt = nc.dram_tensor("xn_t", [SH, D], F32, kind="ExternalOutput").ap()
    o_xnfm = nc.dram_tensor("xnfm", [8, 128, SH], BF16, kind="ExternalOutput").ap()
    o_q = nc.dram_tensor("qo", [SH, 512], F32, kind="ExternalOutput").ap()
    o_k = nc.dram_tensor("ko", [SH, 512], F32, kind="ExternalOutput").ap()
    o_v = nc.dram_tensor("vo", [SH, D], F32, kind="ExternalOutput").ap()
    o_sd = nc.dram_tensor("sdo", [SH, 512], F32, kind="ExternalOutput").ap()
    o_ab = nc.dram_tensor("abo", [SH, 512], F32, kind="ExternalOutput").ap()
    o_bb = nc.dram_tensor("bbo", [SH, 8], F32, kind="ExternalOutput").ap()
    o_gate = nc.dram_tensor("gate", [SH, D], F32, kind="ExternalOutput").ap()
    o_xnmax = nc.dram_tensor("xnmax", [128, 1], F32, kind="ExternalOutput").ap()
    h0_fm = nc.dram_tensor("h0fm", [8, 128, SH], BF16)
    hal_fm = nc.dram_tensor("halfm", [8, 128, 3], BF16)
    h_fm = nc.dram_tensor("hfm", [8, 128, SH], BF16)

    es = contextlib.ExitStack()
    A = lambda name, shape, dt=F32: alloc_sb(es, nc, name, shape, dt)
    P = lambda name, shape, dt=F32: alloc_ps(es, nc, name, shape, dt)
    wbig = A("wbig", [128, 8 * (512 + 512 + 1024 + 512 + 64 + 8 + 64)], BF16)
    wadb = A("wadb", [64, 512], BF16)
    wu2b = A("wu2b", [64, D], BF16)
    idb = A("idb", [128, 128], BF16)
    wnb = A("wnb", [128, D], F32)
    brow = A("brow", [1, D], F32)
    baub = A("baub", [128, 64], F32)
    badb = A("badb", [128, 512], F32)
    bbetb = A("bbetb", [128, 8], F32)
    bu1b = A("bu1b", [128, 64], F32)
    bu2b = A("bu2b", [128, D], F32)
    xt = [A(f"xt{i}", [128, 4 * 1024], F32) for i in range(2)]
    xnt = A("xnt", [128, 1024], F32)
    h0t = A("h0t", [128, 1024], F32)
    xnb = A("xnb", [128, 1024], BF16)
    hfs = [A(f"hfs{i}", [128, 8 * 512], BF16) for i in range(2)]
    sq = A("sq", [128, 1024], F32)
    st = A("st", [128, 8], F32)
    cst = A("cst", [128, 4], F32)
    xm = A("xm", [128, 2], F32)
    prj = A("prj", [128, 1024], F32)
    a1t = A("a1t", [128, 64], F32)
    a1b = A("a1b", [64, 128], BF16)
    convw = A("convw", [128, 8 * 5], F32)
    hcin = A("hcin", [128, SH + 3], BF16)
    hcon = A("hcon", [128, SH], F32)
    hcob = A("hcob", [128, SH], BF16)
    ones_r = A("ones_r", [1, 128], F32)
    ps_m = [P(f"ps_m{i}", [128, 512]) for i in range(2)]
    ps_tr = P("ps_tr", [128, 128], BF16)
    ps_s = P("ps_s", [128, 64])
    ps_b = P("ps_b", [128, 512])

    pb = PB(nc)
    op1(pb, "vector", "memset", ones_r[:], 1.0)
    op1(pb, "vector", "memset", cst[:, 0:1], 1e-5)
    op1(pb, "vector", "memset", cst[:, 1:2], 1e-6)
    op1(pb, "vector", "memset", xm[:], 0.0)
    pb.dma(idb[:], ident, engine="gdma")
    WOFF = [0, 512, 1024, 2048, 2560, 2624, 2632]
    WW = 2696
    for dc in range(8):
        sl = slice(dc * 128, (dc + 1) * 128)
        pb.dma(wbig[:, dc * WW + 0: dc * WW + 512], wq[sl], engine="gdma")
        pb.dma(wbig[:, dc * WW + 512: dc * WW + 1024], wk[sl], engine="gdma")
        pb.dma(wbig[:, dc * WW + 1024: dc * WW + 2048], wv[sl], engine="gdma")
        pb.dma(wbig[:, dc * WW + 2048: dc * WW + 2560], wsp[sl], engine="gdma")
        pb.dma(wbig[:, dc * WW + 2560: dc * WW + 2624], wau[sl], engine="gdma")
        pb.dma(wbig[:, dc * WW + 2624: dc * WW + 2632], wbet[sl], engine="gdma")
        pb.dma(wbig[:, dc * WW + 2632: dc * WW + 2696], wu1[sl], engine="gdma")
    pb.dma(wadb[:], wad, engine="gdma")
    pb.dma(wu2b[:], wu2, engine="gdma")
    for ch in range(8):
        pb.dma(convw[:, ch * 5:ch * 5 + 4], cw4[ch], engine="sync")
        pb.dma(convw[:, ch * 5 + 4:ch * 5 + 5], cb4[ch], engine="sync")
    pb.dma(brow[:], wnorm, engine="sync")
    rowbc(pb, wnb[:, 0:512], brow[0:1, 0:512], ones_r, ps_b)
    rowbc(pb, wnb[:, 512:1024], brow[0:1, 512:1024], ones_r, ps_b)
    pb.dma(brow[0:1, 0:64], bau, engine="sync")
    rowbc(pb, baub[:], brow[0:1, 0:64], ones_r, ps_s)
    pb.dma(brow[0:1, 0:512], bad, engine="sync")
    rowbc(pb, badb[:], brow[0:1, 0:512], ones_r, ps_b)
    pb.dma(brow[0:1, 0:8], bbet, engine="sync")
    mm_acc(pb, ps_s[:, 0:8], [(ones_r, brow[0:1, 0:8])])
    cp(pb, bbetb[:], ps_s[:, 0:8])
    pb.dma(brow[0:1, 0:64], bu1, engine="sync")
    rowbc(pb, bu1b[:], brow[0:1, 0:64], ones_r, ps_s)
    pb.dma(brow[:], bu2, engine="sync")
    rowbc(pb, bu2b[:, 0:512], brow[0:1, 0:512], ones_r, ps_b)
    rowbc(pb, bu2b[:, 512:1024], brow[0:1, 512:1024], ones_r, ps_b)

    hal = A("hal", [3, 1024], F32)
    hsq = A("hsq", [3, 1024], F32)
    hst = A("hst", [3, 8], F32)
    pb.dma(hal[:], xhal, engine="sync")
    layernorm128(pb, hal[:], hal[:], hsq, hst, cst[0:3, 0:1])
    act(pb, hsq[:], hal[:], AF.Square)
    red(pb, hst[:, 0:1], hsq[:], ALU.add)
    act(pb, hst[:, 1:2], hst[:, 0:1], AF.Ln, bias=cst[0:3, 1:2], scale=1.0 / 1024.0)
    act(pb, hst[:, 1:2], hst[:, 1:2], AF.Exp, scale=-0.5)
    ts(pb, hal[:], hal[:], hst[:, 1:2], None, ALU.mult)
    tt(pb, hal[:], hal[:], wnb[0:3, :], ALU.mult)
    for dc in range(8):
        transpose_op(pb, ps_tr[:, 0:3], hal[:, dc * 128:(dc + 1) * 128].bitcast(F32), idb[0:3, 0:3]) if False else None
    halb = A("halb", [3, 1024], BF16)
    cp(pb, halb[:], hal[:])
    for dc in range(8):
        transpose_op(pb, ps_tr[:, 0:3], halb[:, dc * 128:(dc + 1) * 128], idb[0:3, 0:3])
        cp(pb, hcin[:, 0:3], ps_tr[:, 0:3])
        pb.dma(hal_fm.ap()[dc], hcin[:, 0:3], engine="sync")

    NSLAB = SH // 512
    for s in range(NSLAB):
        xts = xt[s % 2]
        for j in range(4):
            pb.dma(xts[:, j * 1024:(j + 1) * 1024],
                   x[s * 512 + j * 128: s * 512 + (j + 1) * 128, :], engine="sync")
        for j in range(4):
            xj = xts[:, j * 1024:(j + 1) * 1024]
            layernorm128(pb, xnt[:], xj, sq, st, cst[:, 0:1])
            pb.dma(o_xnt[s * 512 + j * 128: s * 512 + (j + 1) * 128, :], xnt[:],
                   engine="sync")
            red(pb, xm[:, 1:2], xnt[:], ALU.max, absval=True)
            tt(pb, xm[:, 0:1], xm[:, 0:1], xm[:, 1:2], ALU.max)
            cp(pb, xnb[:], xnt[:])
            for dc in range(8):
                transpose_op(pb, ps_tr, xnb[:, dc * 128:(dc + 1) * 128], idb)
                cp(pb, hcin[:, 4:132], ps_tr)
                pb.dma(o_xnfm[dc, :, s * 512 + j * 128: s * 512 + (j + 1) * 128],
                       hcin[:, 4:132], engine="sync")
            act(pb, sq, xnt[:], AF.Square)
            red(pb, st[:, 0:1], sq, ALU.add)
            act(pb, st[:, 1:2], st[:, 0:1], AF.Ln, bias=cst[:, 1:2],
                scale=1.0 / 1024.0)
            act(pb, st[:, 1:2], st[:, 1:2], AF.Exp, scale=-0.5)
            ts(pb, h0t[:], xnt[:], st[:, 1:2], None, ALU.mult)
            tt(pb, h0t[:], h0t[:], wnb[:], ALU.mult)
            cp(pb, xnb[:], h0t[:])
            for dc in range(8):
                transpose_op(pb, ps_tr, xnb[:, dc * 128:(dc + 1) * 128], idb)
                cp(pb, hcin[:, 4:132], ps_tr)
                pb.dma(h0_fm.ap()[dc, :, s * 512 + j * 128: s * 512 + (j + 1) * 128],
                       hcin[:, 4:132], engine="sync")
    pb.dma(o_xnmax, xm[:, 0:1], engine="sync")
    for ch in range(8):
        pb.dma(hcin[:, 0:3], hal_fm.ap()[ch], engine="sync")
        pb.dma(hcin[:, 3:SH + 3], h0_fm.ap()[ch], engine="sync")
        ts(pb, hcon[:], hcin[:, 0:SH], convw[:, ch * 5:ch * 5 + 1],
           convw[:, ch * 5 + 4:ch * 5 + 5], ALU.mult, ALU.add)
        for j in range(1, 4):
            stt(pb, hcon[:], hcin[:, j:SH + j], convw[:, ch * 5 + j:ch * 5 + j + 1],
                hcon[:], ALU.mult, ALU.add)
        act(pb, hcob[:], hcon[:], AF.Silu)
        pb.dma(h_fm.ap()[ch], hcob[:], engine="sync")
    for s in range(NSLAB):
        hf = hfs[s % 2]
        pb.dma(hf[:].rearrange("p (c t) -> p c t", c=8),
               h_fm.ap()[:, :, s * 512:(s + 1) * 512].transpose([1, 0, 2]),
               engine="sync")
        for j in range(4):
            t4 = slice(s * 512 + j * 128, s * 512 + (j + 1) * 128)
            hsl = [hf[:, dc * 512 + j * 128: dc * 512 + j * 128 + 128]
                   for dc in range(8)]
            for pi, (woff, dst) in enumerate(((0, o_q), (512, o_k))):
                pm = ps_m[pi % 2]
                mm_acc(pb, pm, [(hsl[dc],
                                 wbig[:, dc * WW + woff: dc * WW + woff + 512])
                                for dc in range(8)])
                cp(pb, prj[:, 0:512], pm)
                act(pb, sq[:, 0:512], prj[:, 0:512], AF.Square)
                red(pb, st[:, 0:8], sq[:, 0:512].rearrange(
                    "p (h d) -> p h d", d=64), ALU.add)
                act(pb, st[:, 0:8], st[:, 0:8], AF.Sqrt)
                ts(pb, st[:, 0:8], st[:, 0:8], 1e-6, None, ALU.add)
                op1(pb, "vector", "reciprocal", st[:, 0:8], st[:, 0:8])
                tt(pb, prj[:, 0:512].rearrange("p (h d) -> p h d", d=64),
                   prj[:, 0:512].rearrange("p (h d) -> p h d", d=64),
                   st[:, 0:8].unsqueeze(2).broadcast_to([128, 8, 64]), ALU.mult)
                pb.dma(dst[t4, :], prj[:, 0:512], engine="sync")
            for half in range(2):
                pm = ps_m[half]
                mm_acc(pb, pm, [
                    (hsl[dc], wbig[:, dc * WW + 1024 + half * 512:
                                   dc * WW + 1024 + (half + 1) * 512])
                    for dc in range(8)])
                cp(pb, prj[:, half * 512:(half + 1) * 512], pm)
            pb.dma(o_v[t4, :], prj[:], engine="sync")
            pm = ps_m[0]
            mm_acc(pb, pm, [(hsl[dc], wbig[:, dc * WW + 2048: dc * WW + 2560])
                            for dc in range(8)])
            cp(pb, prj[:, 0:512], pm)
            pb.dma(o_sd[t4, :], prj[:, 0:512], engine="sync")
            mm_acc(pb, ps_s, [(hsl[dc], wbig[:, dc * WW + 2560: dc * WW + 2624])
                              for dc in range(8)])
            tt(pb, a1t[:], ps_s, baub[:], ALU.add)
            act(pb, a1t[:], a1t[:], AF.Silu)
            cp(pb, sq[:, 0:64].bitcast(F32), a1t[:]) if False else None
            cp(pb, xnb[:, 0:64], a1t[:])
            transpose_op(pb, ps_tr[:, 0:64].bitcast(BF16), xnb[:, 0:64], idb) \
                if False else None
            transpose_op(pb, ps_tr, xnb[:, 0:128], idb)
            cp(pb, a1b[:], ps_tr[0:64, 0:128])
            pm = ps_m[1]
            mm_acc(pb, pm, [(a1b[:], wadb[:])])
            tt(pb, prj[:, 0:512], pm, badb[:], ALU.add)
            pb.dma(o_ab[t4, :], prj[:, 0:512], engine="sync")
            mm_acc(pb, ps_s[:, 0:8], [(hsl[dc], wbig[:, dc * WW + 2624: dc * WW + 2632])
                              for dc in range(8)])
            tt(pb, prj[:, 0:8], ps_s[:, 0:8], bbetb[:], ALU.add)
            pb.dma(o_bb[t4, :], prj[:, 0:8], engine="sync")
            xsl = [None] * 8
            pb.dma(xnb[:].rearrange("p (c t) -> p c t", c=8),
                   o_xnfm[:, :, t4].transpose([1, 0, 2]), engine="sync")
            mm_acc(pb, ps_s, [(xnb[:, dc * 128:(dc + 1) * 128],
                               wbig[:, dc * WW + 2632: dc * WW + 2696])
                              for dc in range(8)])
            tt(pb, a1t[:], ps_s, bu1b[:], ALU.add)
            act(pb, a1t[:], a1t[:], AF.Silu)
            cp(pb, xnb[:, 0:128], a1t[:].broadcast_to([128, 64])) \
                if False else None
            cp(pb, xnb[:, 0:64], a1t[:])
            op1(pb, "vector", "memset", xnb[:, 64:128], 0.0)
            transpose_op(pb, ps_tr, xnb[:, 0:128], idb)
            cp(pb, a1b[:], ps_tr[0:64, 0:128])
            for half in range(2):
                pm = ps_m[half]
                mm_acc(pb, pm, [(a1b[:], wu2b[:, half * 512:(half + 1) * 512])])
                tt(pb, prj[:, half * 512:(half + 1) * 512], pm,
                   bu2b[:, half * 512:(half + 1) * 512], ALU.add)
            act(pb, prj[:], prj[:], AF.Sigmoid)
            pb.dma(o_gate[t4, :], prj[:], engine="sync")
    pb.emit()
    es.close()
    return nc


def build_L2b(dbg=False):
    nc = bass.Bass("TRN2", target_bir_lowering=False, debug=False)
    qsl = nc.dram_tensor("qsl", [128, T], F32, kind="ExternalInput").ap()
    ksl = nc.dram_tensor("ksl", [128, T], F32, kind="ExternalInput").ap()
    sdsl = nc.dram_tensor("sdsl", [128, T], F32, kind="ExternalInput").ap()
    absl = nc.dram_tensor("absl", [128, T], F32, kind="ExternalInput").ap()
    bbrep = nc.dram_tensor("bbrep", [128, T], F32, kind="ExternalInput").ap()
    vsl = nc.dram_tensor("vsl", [2, T, 128], F32, kind="ExternalInput").ap()
    asp = nc.dram_tensor("asp", [128, 1], F32, kind="ExternalInput").ap()
    bsp = nc.dram_tensor("bsp", [128, 1], F32, kind="ExternalInput").ap()
    whn = nc.dram_tensor("whn", [2, 1, 128], F32, kind="ExternalInput").ap()
    pones = nc.dram_tensor("pones", [128, 2], F32, kind="ExternalInput").ap()
    pind = nc.dram_tensor("pind", [2, 128], F32, kind="ExternalInput").ap()
    maskL = nc.dram_tensor("maskL", [32, 32], F32, kind="ExternalInput").ap()
    maskU = nc.dram_tensor("maskU", [32, 32], F32, kind="ExternalInput").ap()
    maskUI = nc.dram_tensor("maskUI", [32, 32], F32, kind="ExternalInput").ap()
    ident = nc.dram_tensor("ident", [128, 128], F32, kind="ExternalInput").ap()
    o_y = nc.dram_tensor("y", [2, T, 128], F32, kind="ExternalOutput").ap()
    if dbg:
        o_hard = nc.dram_tensor("dbg_hard", [128, 2048], F32, kind="ExternalOutput").ap()
        o_beta = nc.dram_tensor("dbg_beta", [128, 2048], F32, kind="ExternalOutput").ap()
        o_L = nc.dram_tensor("dbg_L", [128, 2048], F32, kind="ExternalOutput").ap()
        o_p = nc.dram_tensor("dbg_p", [128, 2048], F32, kind="ExternalOutput").ap()
    btscr = nc.dram_tensor("btscr", [2, T], F32)

    TSL = 2048
    NSL = T // TSL
    NCH = TSL // 32
    es = contextlib.ExitStack()
    A = lambda name, shape, dt=F32: alloc_sb(es, nc, name, shape, dt)
    P = lambda name, shape, dt=F32: alloc_ps(es, nc, name, shape, dt)
    idb = A("idb", [128, 128], BF16)
    mLb = A("mLb", [32, 32], BF16)
    mUb = A("mUb", [32, 32], BF16)
    mUIb = A("mUIb", [32, 32], BF16)
    ponesb = A("ponesb", [128, 2], F32)
    pindb = A("pindb", [2, 128], F32)
    aspc = A("aspc", [128, 1], F32)
    bspc = A("bspc", [128, 1], F32)
    whnb = A("whnb", [32, 2 * 128], F32)
    sdb = A("sdb", [128, TSL], F32)
    qb = A("qb", [128, TSL], F32)
    kb = A("kb", [128, TSL], F32)
    abb = A("abb", [128, TSL], F32)
    bbb = A("bbb", [128, TSL], F32)
    psb = A("psb", [128, TSL], F32)
    hardb = A("hardb", [128, TSL], F32)
    t1b = A("t1b", [128, TSL], F32)
    t2b = A("t2b", [128, TSL], F32)
    lab = A("lab", [128, TSL], F32)
    cpad = [A(f"cpad{i}", [128, NCH * 32], F32) for i in range(2)]
    Lb = A("Lb", [128, TSL], F32)
    mcol = A("mcol", [128, NCH], F32)
    emcol = A("emcol", [128, NCH], F32)
    elcc = A("elcc", [128, NCH], F32)
    beta = A("beta", [128, TSL], F32)
    k0b = A("k0b", [128, TSL], BF16)
    q0b = A("q0b", [128, TSL], BF16)
    kMb = A("kMb", [128, TSL], BF16)
    qMb = A("qMb", [128, TSL], BF16)
    ktb = A("ktb", [128, TSL], BF16)
    kbarb = A("kbarb", [128, TSL], BF16)
    rcar = A("rcar", [128, 1], F32)
    rsave = A("rsave", [128, 1], F32)
    powr = A("powr", [128, TSL], F32)
    zero1 = A("zero1", [128, TSL], F32)
    c09 = A("c09", [128, TSL], F32)
    kM_s = A("kM_s", [64, 32], BF16)
    kt_s = A("kt_s", [64, 32], BF16)
    k0_s = A("k0_s", [64, 32], BF16)
    qM_s = A("qM_s", [64, 32], BF16)
    q0_s = A("q0_s", [64, 32], BF16)
    kb_s = A("kb_s", [64, 32], BF16)
    elc_s = A("elc_s", [64, 1], F32)
    a_sb = A("a_sb", [32, 32], BF16)
    at_sb = A("at_sb", [32, 32], BF16)
    a2 = A("a2", [32, 32], BF16)
    a2t = A("a2t", [32, 32], BF16)
    a4 = A("a4", [32, 32], BF16)
    a4t = A("a4t", [32, 32], BF16)
    a8 = A("a8", [32, 32], BF16)
    a8t = A("a8t", [32, 32], BF16)
    a16t = A("a16t", [32, 32], BF16)
    vck = A("vck", [32, 128], BF16)
    bcol = A("bcol", [32, 1], F32)
    Rb = A("Rb", [32, 128], BF16)
    Ub = A("Ub", [32, 128], BF16)
    pmT = A("pmT", [32, 32], BF16)
    kbT = A("kbT", [32, 64], BF16)
    S0f = [A(f"S0f{p}", [64, 128], F32) for p in range(2)]
    S0b = [A(f"S0b{p}", [64, 128], BF16) for p in range(2)]
    ysc = A("ysc", [32, 128], F32)
    ysq = A("ysq", [32, 128], F32)
    yst = A("yst", [32, 4], F32)
    brow2 = A("brow2", [1, 128], F32)
    ones_r = A("ones_r", [1, 32], F32)
    ps_a = P("ps_a", [32, 32])
    ps_r = P("ps_r", [32, 128])
    ps_o = P("ps_o", [32, 128])
    ps_su = P("ps_su", [64, 128])
    ps_tr = P("ps_tr", [128, 128], BF16)
    ps_p2 = [P("ps_p2", [2, 512])] * 2
    ps_pb = [P("ps_pb", [128, 512])] * 2

    pb = PB(nc)
    pb.dma(idb[:], ident, engine="gdma")
    pb.dma(mLb[:], maskL, engine="gdma")
    pb.dma(mUb[:], maskU, engine="gdma")
    pb.dma(mUIb[:], maskUI, engine="gdma")
    pb.dma(ponesb[:], pones, engine="gdma")
    pb.dma(pindb[:], pind, engine="gdma")
    pb.dma(aspc[:], asp, engine="sync")
    pb.dma(bspc[:], bsp, engine="sync")
    op1(pb, "vector", "memset", ones_r[:], 1.0)
    op1(pb, "vector", "memset", zero1[:], 0.0)
    op1(pb, "vector", "memset", c09[:], 0.9)
    op1(pb, "vector", "memset", rcar[:], 0.0)

    for p in range(2):
        pb.dma(brow2[:], whn[p], engine="sync")
        def f(eng, _o=ps_r[:], _a=ones_r[:], _b=brow2[:]):
            return eng.matmul(_o, _a, _b, start=True, stop=True)
        pb.add("tensor", f, reads=["ones_r", "brow2"], writes=["ps_r"])
        cp(pb, whnb[:, p * 128:(p + 1) * 128], ps_r)
        op1(pb, "vector", "memset", S0f[p][:], 0.0)
        op1(pb, "vector", "memset", S0b[p][:], 0.0)

    def scanop(dst, d0, d1, init):
        def f(eng, _o=PB._ap(dst), _a=PB._ap(d0), _b=PB._ap(d1), _i=init):
            return eng.tensor_tensor_scan(_o, _a, _b, _i, ALU.mult, ALU.add)
        pb.add("vector", f, reads=[PB._key(d0), PB._key(d1)],
               writes=[PB._key(dst)])

    scanop(powr, c09, zero1, 1.0)

    for s in range(NSL):
        t0 = s * TSL
        pb.dma(sdb[:], sdsl[:, t0:t0 + TSL], engine="sync")
        pb.dma(qb[:], qsl[:, t0:t0 + TSL], engine="sync")
        pb.dma(kb[:], ksl[:, t0:t0 + TSL], engine="sync")
        pb.dma(abb[:], absl[:, t0:t0 + TSL], engine="sync")
        pb.dma(bbb[:], bbrep[:, t0:t0 + TSL], engine="sync")
        scanop(psb, c09, sdb, 0.0)
        stt(pb, psb[:], powr[:], rcar[:], psb[:], ALU.mult, ALU.add)
        cp(pb, rcar[:], psb[:, TSL - 1:TSL])
        ts(pb, psb[:], psb[:], -0.5, 1.0 / 0.45, ALU.add, ALU.mult)
        if s == 0:
            op1(pb, "vector", "memset", t1b[:, 0:1], 0.0)
        else:
            cp(pb, t1b[:, 0:1], rsave[:])
        for t in range(TSL):
            tt(pb, hardb[:, t:t + 1], psb[:, t:t + 1], t1b[:, 0:1], ALU.is_gt)
            stt(pb, t1b[:, 0:1], t1b[:, 0:1], 0.9, hardb[:, t:t + 1],
                ALU.mult, ALU.add)
        cp(pb, rsave[:], t1b[:, 0:1])
        ts(pb, t1b[:], hardb[:], bspc[:], None, ALU.mult)
        for nh in range(4):
            slc = slice(nh * 512, (nh + 1) * 512)
            pp2, ppb = ps_p2[nh % 2], ps_pb[nh % 2]
            def f(eng, _o=pp2[:], _a=ponesb[:], _b1=t1b[:, slc], _b2=hardb[:, slc]):
                return None
            mm_acc(pb, pp2, [(ponesb[:], t1b[:, slc])])
            cp(pb, t2b[0:2, slc], pp2)
            mm_acc(pb, ppb, [(pindb[:], t2b[0:2, slc])])
            cp(pb, t1b[:, slc], ppb)
        for nh in range(4):
            slc = slice(nh * 512, (nh + 1) * 512)
            pp2, ppb = ps_p2[nh % 2], ps_pb[nh % 2]
            mm_acc(pb, pp2, [(ponesb[:], hardb[:, slc])])
            cp(pb, t2b[0:2, slc], pp2)
            mm_acc(pb, ppb, [(pindb[:], t2b[0:2, slc])])
            cp(pb, t2b[:, slc], ppb)
        ts(pb, t2b[:], t2b[:], 0.5, None, ALU.is_gt)
        tt(pb, beta[:], bbb[:], t1b[:], ALU.add)
        act(pb, beta[:], beta[:], AF.Sigmoid)
        tt(pb, beta[:], beta[:], t2b[:], ALU.mult)
        for p in range(2):
            pb.dma(btscr.ap()[p, t0:t0 + TSL].rearrange("t -> () t"),
                   beta[p * 64:p * 64 + 1, :], engine="sync")
        stt(pb, lab[:], hardb[:], aspc[:], abb[:], ALU.mult, ALU.add)
        ts(pb, t1b[:], lab[:], 0.0, None, ALU.min)
        act(pb, lab[:], lab[:], AF.Abs)
        act(pb, lab[:], lab[:], AF.Exp, scale=-1.0)
        ts(pb, lab[:], lab[:], 1.0, None, ALU.add)
        act(pb, lab[:], lab[:], AF.Ln)
        tt(pb, lab[:], t1b[:], lab[:], ALU.subtract)
        tt(pb, lab[:], lab[:], t2b[:], ALU.mult)
        if dbg and s == 0:
            pb.dma(o_hard, hardb[:], engine="sync")
            pb.dma(o_beta, beta[:], engine="sync")
            pb.dma(o_p, psb[:], engine="sync")
        cp(pb, cpad[0][:], lab[:])
        cur = 0
        for d in (1, 2, 4, 8, 16):
            nxt = 1 - cur
            c3c = cpad[cur][:].rearrange("p (c q) -> p c q", q=32)
            c3n = cpad[nxt][:].rearrange("p (c q) -> p c q", q=32)
            def fcp(eng, _o=c3n[:, :, 0:d], _a=c3c[:, :, 0:d]):
                return eng.tensor_copy(_o, _a)
            pb.add("vector", fcp, reads=[f"cpad{cur}"], writes=[f"cpad{nxt}"])
            def fad(eng, _o=c3n[:, :, d:32], _a=c3c[:, :, d:32],
                    _b=c3c[:, :, 0:32 - d]):
                return eng.tensor_tensor(_o, _a, _b, ALU.add)
            pb.add("vector", fad, reads=[f"cpad{cur}"], writes=[f"cpad{nxt}"])
            cur = nxt
        cp(pb, Lb[:], cpad[cur][:])
        if dbg and s == 0:
            pb.dma(o_L, Lb[:], engine="sync")
        ts(pb, mcol[:], Lb[:].rearrange("p (c q) -> p c q", q=32)[:, :, 31], 0.5)
        act(pb, emcol[:], mcol[:], AF.Exp)
        act(pb, elcc[:], mcol[:], AF.Exp, scale=2.0)
        act(pb, t1b[:], Lb[:], AF.Exp)
        tt(pb, k0b[:], kb[:], t1b[:], ALU.mult)
        tt(pb, q0b[:], qb[:], t1b[:], ALU.mult)
        def fsub(eng, _o=t1b[:], _a=Lb[:].rearrange("p (c q) -> p c q", q=32),
                 _b=mcol[:].unsqueeze(2).broadcast_to([128, NCH, 32])):
            return eng.tensor_tensor(_o.rearrange("p (c q) -> p c q", q=32),
                                     _a, _b, ALU.subtract)
        pb.add("vector", fsub, reads=["Lb", "mcol"], writes=["t1b"])
        ts(pb, t1b[:], -80.0, 80.0, ALU.max, ALU.min) if False else None
        ts(pb, t1b[:], t1b[:], -80.0, 80.0, ALU.max, ALU.min)
        act(pb, t2b[:], t1b[:], AF.Exp)
        tt(pb, kMb[:], kb[:], t2b[:], ALU.mult)
        tt(pb, kMb[:], kMb[:], beta[:], ALU.mult)
        tt(pb, qMb[:], qb[:], t2b[:], ALU.mult)
        act(pb, t2b[:], t1b[:], AF.Exp, scale=-1.0)
        tt(pb, ktb[:], kb[:], t2b[:], ALU.mult)
        def fkb(eng, _o=kbarb[:].rearrange("p (c q) -> p c q", q=32),
                _a=ktb[:].rearrange("p (c q) -> p c q", q=32),
                _b=emcol[:].unsqueeze(2).broadcast_to([128, NCH, 32])):
            return eng.tensor_tensor(_o, _a, _b, ALU.mult)
        pb.add("vector", fkb, reads=["ktb", "emcol"], writes=["kbarb"])
        for c in range(NCH):
            cc = slice(c * 32, (c + 1) * 32)
            gt = t0 + c * 32
            for p in range(2):
                rw = slice(p * 64, (p + 1) * 64)
                cp(pb, kM_s[:], kMb[rw, cc])
                cp(pb, kt_s[:], ktb[rw, cc])
                cp(pb, k0_s[:], k0b[rw, cc])
                cp(pb, qM_s[:], qMb[rw, cc])
                cp(pb, q0_s[:], q0b[rw, cc])
                cp(pb, kb_s[:], kbarb[rw, cc])
                mm_acc(pb, ps_a, [(kM_s[:], kt_s[:])])
                tt(pb, a_sb[:], ps_a, mLb[:], ALU.mult)
                mm_acc(pb, ps_a, [(kt_s[:], kM_s[:])])
                tt(pb, at_sb[:], ps_a, mUb[:], ALU.mult)
                for (srcT, src, dstT, dst) in ((at_sb, a_sb, a2t, a2),
                                               (a2t, a2, a4t, a4),
                                               (a4t, a4, a8t, a8)):
                    mm_acc(pb, ps_a, [(src, srcT)])
                    cp(pb, dstT[:], ps_a)
                    mm_acc(pb, ps_a, [(srcT, src)])
                    cp(pb, dst[:], ps_a)
                mm_acc(pb, ps_a, [(a8, a8t)])
                cp(pb, a16t[:], ps_a)
                pb.dma(vck[:], vsl[p, gt:gt + 32, :], engine="gdma")
                pb.dma(bcol[:], btscr.ap()[p, gt:gt + 32].rearrange(
                    "(t one) -> t one", one=1), engine="sync")
                mm_acc(pb, ps_r, [(k0_s[:], S0b[p][:])])
                tt(pb, Rb[:], vck[:], ps_r, ALU.subtract)
                ts(pb, Rb[:], Rb[:], bcol[:], None, ALU.mult)
                cp(pb, Ub[:], Rb[:])
                for (fT, sgn) in ((a16t, 1.0), (a8t, 1.0), (a4t, 1.0),
                                  (a2t, 1.0), (at_sb, -1.0)):
                    mm_acc(pb, ps_r, [(fT[:], Ub[:])])
                    if sgn > 0:
                        tt(pb, Ub[:], Ub[:], ps_r, ALU.add)
                    else:
                        tt(pb, Ub[:], Ub[:], ps_r, ALU.subtract)
                mm_acc(pb, ps_a, [(kt_s[:], qM_s[:])])
                tt(pb, pmT[:], ps_a, mUIb[:], ALU.mult)
                def fo(eng, _o=ps_o[:], _a=q0_s[:], _b=S0b[p][:],
                       _a2=pmT[:], _b2=Ub[:]):
                    eng.matmul(_o, _a, _b, start=True, stop=False)
                    return eng.matmul(_o, _a2, _b2, start=False, stop=True)
                pb.add("tensor", fo, reads=["q0_s", f"S0b{p}", "pmT", "Ub"],
                       writes=["ps_o"])
                cp(pb, ysc[:], ps_o)
                act(pb, ysq[:], ysc[:], AF.Square)
                red(pb, yst[:, 0:1], ysq[:], ALU.add)
                ts(pb, yst[:, 0:1], yst[:, 0:1], 1.0 / 128.0, None, ALU.mult)
                ts(pb, yst[:, 0:1], yst[:, 0:1], 1e-6, None, ALU.add)
                act(pb, yst[:, 1:2], yst[:, 0:1], AF.Ln)
                act(pb, yst[:, 1:2], yst[:, 1:2], AF.Exp, scale=-0.5)
                ts(pb, ysc[:], ysc[:], yst[:, 1:2], None, ALU.mult)
                tt(pb, ysc[:], ysc[:], whnb[:, p * 128:(p + 1) * 128], ALU.mult)
                pb.dma(o_y[p, gt:gt + 32, :], ysc[:], engine="sync")
                transpose_op(pb, ps_tr[0:32, 0:64], kb_s[:], idb[0:64, 0:64])
                cp(pb, kbT[:], ps_tr[0:32, 0:64])
                mm_acc(pb, ps_su, [(kbT[:], Ub[:])])
                cp(pb, elc_s[:], elcc[rw, c:c + 1])
                stt(pb, S0f[p][:], S0f[p][:],
                    elc_s[:], ps_su, ALU.mult, ALU.add)
                cp(pb, S0b[p][:], S0f[p][:])
    pb.emit()
    es.close()
    return nc


def build_L2c():
    nc = bass.Bass("TRN2", target_bir_lowering=False, debug=False)
    y_t = nc.dram_tensor("y_t", [SH, D], F32, kind="ExternalInput").ap()
    gate = nc.dram_tensor("gate", [SH, D], F32, kind="ExternalInput").ap()
    xn_t = nc.dram_tensor("xn_t", [SH, D], F32, kind="ExternalInput").ap()
    wout = nc.dram_tensor("wout", [D, D], F32, kind="ExternalInput").ap()
    wffn = nc.dram_tensor("wffn", [1, D], F32, kind="ExternalInput").ap()
    ff1 = nc.dram_tensor("ff1", [D, 4096], F32, kind="ExternalInput").ap()
    ff3 = nc.dram_tensor("ff3", [D, 4096], F32, kind="ExternalInput").ap()
    ff2 = nc.dram_tensor("ff2", [4096, D], F32, kind="ExternalInput").ap()
    ident = nc.dram_tensor("ident", [128, 128], F32, kind="ExternalInput").ap()
    o_s2 = nc.dram_tensor("s2", [SH, D], F32, kind="ExternalOutput").ap()
    o_ya = nc.dram_tensor("yadbg", [512, D], F32, kind="ExternalOutput").ap()
    o_h2 = nc.dram_tensor("h2dbg", [128, 8 * 512], F32, kind="ExternalOutput").ap()
    ya_scr = nc.dram_tensor("ya_scr", [512, D], F32)

    es = contextlib.ExitStack()
    A = lambda name, shape, dt=F32: alloc_sb(es, nc, name, shape, dt)
    P = lambda name, shape, dt=F32: alloc_ps(es, nc, name, shape, dt)
    w1b = A("w1b", [128, 8 * 4096], BF16)
    w3b = A("w3b", [128, 8 * 4096], BF16)
    w2s = [A(f"w2s{i}", [128, 512], BF16) for i in range(4)]
    idb = A("idb", [128, 128], BF16)
    wfb = A("wfb", [128, D], F32)
    brow = A("brow", [1, D], F32)
    yt = A("yt", [128, D], F32)
    gt_ = A("gt_", [128, D], F32)
    xnt = A("xnt", [128, D], F32)
    ya = A("ya", [128, D], F32)
    h2b = A("h2b", [128, D], BF16)
    h2fm = A("h2fm", [128, 8 * 512], BF16)
    swz = A("swz", [128, 32 * 512], BF16)
    sq = A("sq", [128, 1024], F32)
    st = A("st", [128, 8], F32)
    cst = A("cst", [128, 2], F32)
    ones_r = A("ones_r", [1, 128], F32)
    ps_m = [P(f"ps_m{i}", [128, 512]) for i in range(2)]
    ps_n = [P(f"ps_n{i}", [128, 512]) for i in range(2)]
    ps_tr = P("ps_tr", [128, 128], BF16)
    ps_b = P("ps_b", [128, 512])

    pb = PB(nc)
    op1(pb, "vector", "memset", ones_r[:], 1.0)
    op1(pb, "vector", "memset", cst[:, 0:1], 1e-6)
    pb.dma(idb[:], ident, engine="gdma")
    for dc in range(8):
        sl = slice(dc * 128, (dc + 1) * 128)
        pb.dma(w1b[:, dc * 4096:(dc + 1) * 4096], ff1[sl], engine="gdma")
        pb.dma(w3b[:, dc * 4096:(dc + 1) * 4096], ff3[sl], engine="gdma")
    pb.dma(brow[:], wffn, engine="sync")
    rowbc(pb, wfb[:, 0:512], brow[0:1, 0:512], ones_r, ps_b)
    rowbc(pb, wfb[:, 512:1024], brow[0:1, 512:1024], ones_r, ps_b)

    NSLAB = SH // 512
    for s in range(NSLAB):
        for j in range(4):
            t4 = slice(s * 512 + j * 128, s * 512 + (j + 1) * 128)
            pb.dma(yt[:], y_t[t4, :], engine="sync")
            pb.dma(gt_[:], gate[t4, :], engine="sync")
            pb.dma(xnt[:], xn_t[t4, :], engine="sync")
            cp(pb, h2b[:], yt[:])
            for dc in range(8):
                transpose_op(pb, ps_tr, h2b[:, dc * 128:(dc + 1) * 128], idb)
                cp(pb, h2fm[:, dc * 512 + j * 128: dc * 512 + j * 128 + 128],
                   ps_tr)
        for j in range(4):
            t4 = slice(s * 512 + j * 128, s * 512 + (j + 1) * 128)
            pb.dma(gt_[:], gate[t4, :], engine="sync")
            pb.dma(xnt[:], xn_t[t4, :], engine="sync")
            for half in range(2):
                pm = ps_m[half]
                for dc in range(8):
                    pb.dma(w2s[dc % 4][:],
                           wout[dc * 128:(dc + 1) * 128,
                                half * 512:(half + 1) * 512], engine="gdma")
                    def fmo(eng, _o=pm[:],
                            _a=h2fm[:, dc * 512 + j * 128: dc * 512 + j * 128 + 128],
                            _b=w2s[dc % 4][:], _st=(dc == 0), _sp=(dc == 7)):
                        return eng.matmul(_o, _a, _b, start=_st, stop=_sp,
                                          skip_group_check=True)
                    pb.add("tensor", fmo, reads=["h2fm", f"w2s{dc % 4}"],
                           writes=[f"ps_m{half}"])
                hs = slice(half * 512, (half + 1) * 512)
                tt(pb, ya[:, hs], pm, gt_[:, hs], ALU.mult)
                tt(pb, ya[:, hs], ya[:, hs], xnt[:, hs], ALU.add)
            pb.dma(ya_scr.ap()[j * 128:(j + 1) * 128, :], ya[:], engine="sync")
            if s == 0:
                pb.dma(o_ya[j * 128:(j + 1) * 128, :], ya[:], engine="sync")
            act(pb, sq, ya[:], AF.Square)
            red(pb, st[:, 0:1], sq, ALU.add)
            act(pb, st[:, 1:2], st[:, 0:1], AF.Ln, bias=cst[:, 0:1],
                scale=1.0 / 1024.0)
            act(pb, st[:, 1:2], st[:, 1:2], AF.Exp, scale=-0.5)
            ts(pb, sq, ya[:], st[:, 1:2], None, ALU.mult)
            tt(pb, sq, sq, wfb[:], ALU.mult)
            cp(pb, h2b[:], sq)
            for dc in range(8):
                transpose_op(pb, ps_tr, h2b[:, dc * 128:(dc + 1) * 128], idb)
                cp(pb, h2fm[:, dc * 512 + j * 128: dc * 512 + j * 128 + 128],
                   ps_tr)
        if s == 0:
            def fdbg(eng, _o=o_h2, _i=h2fm[:]):
                return eng.dma_start(out=_o, in_=_i)
            pb.add("gdma", fdbg, reads=["h2fm"], writes=["h2dbg"], kind="dma")
        for fb in range(32):
            pm, pn = ps_m[fb % 2], ps_n[fb % 2]
            mm_acc(pb, pm, [
                (w1b[:, dc * 4096 + fb * 128: dc * 4096 + (fb + 1) * 128],
                 h2fm[:, dc * 512:(dc + 1) * 512]) for dc in range(8)])
            mm_acc(pb, pn, [
                (w3b[:, dc * 4096 + fb * 128: dc * 4096 + (fb + 1) * 128],
                 h2fm[:, dc * 512:(dc + 1) * 512]) for dc in range(8)])
            act(pb, swz[:, fb * 512:(fb + 1) * 512], pm, AF.Silu)
            def fmul(eng, _o=swz[:, fb * 512:(fb + 1) * 512], _b=pn[:]):
                return eng.tensor_tensor(_o, _o, _b, ALU.mult)
            pb.add("vector", fmul, reads=["swz", f"ps_n{fb % 2}"], writes=["swz"])
        for half in range(2):
            pjs = [ps_m[0], ps_m[1], ps_n[0], ps_n[1]]
            for fb in range(32):
                pb.dma(w2s[fb % 4][:],
                       ff2[fb * 128:(fb + 1) * 128, half * 512:(half + 1) * 512],
                       engine="gdma")
                for j in range(4):
                    def fmm(eng, _o=pjs[j][:],
                            _a=swz[:, fb * 512 + j * 128: fb * 512 + j * 128 + 128],
                            _b=w2s[fb % 4][:], _st=(fb == 0), _sp=(fb == 31)):
                        return eng.matmul(_o, _a, _b, start=_st, stop=_sp,
                                          skip_group_check=True)
                    nmj = ("ps_m0", "ps_m1", "ps_n0", "ps_n1")[j]
                    pb.add("tensor", fmm, reads=["swz", f"w2s{fb % 4}"],
                           writes=[nmj])
            for j in range(4):
                t4 = slice(s * 512 + j * 128, s * 512 + (j + 1) * 128)
                pb.dma(ya[:], ya_scr.ap()[j * 128:(j + 1) * 128, :], engine="sync")
                hs = slice(half * 512, (half + 1) * 512)
                tt(pb, yt[:, hs], pjs[j], ya[:, hs], ALU.add)
                pb.dma(o_s2[t4, half * 512:(half + 1) * 512], yt[:, hs],
                       engine="sync")
    pb.emit()
    es.close()
    return nc


def build_L2f():
    nc = bass.Bass("TRN2", target_bir_lowering=False, debug=False)
    xnfm = nc.dram_tensor("xnfm", [8, 128, SH], BF16, kind="ExternalInput").ap()
    xnmax_all = nc.dram_tensor("xnmax_all", [1, 1024], F32, kind="ExternalInput").ap()
    Win = nc.dram_tensor("Win", [D, 4096], F32, kind="ExternalInput").ap()
    binf = nc.dram_tensor("binf", [128, 32], F32, kind="ExternalInput").ap()
    Wout = nc.dram_tensor("Wout", [4096, D], F32, kind="ExternalInput").ap()
    bor = nc.dram_tensor("bor", [1, D], F32, kind="ExternalInput").ap()
    s2 = nc.dram_tensor("s2", [SH, D], F32, kind="ExternalInput").ap()
    x = nc.dram_tensor("x", [SH, D], F32, kind="ExternalInput").ap()
    ident = nc.dram_tensor("ident", [128, 128], F32, kind="ExternalInput").ap()
    o_x3 = nc.dram_tensor("x3", [SH, D], F32, kind="ExternalOutput").ap()
    h_fm = nc.dram_tensor("h_fm2", [32, 128, SH], BF16)
    cbar = nc.dram_tensor("cbar2", [1, 4], F32)
    cbar_o = nc.dram_tensor("cbar_o2", [1, 4], F32, addr_space="Shared")
    wmx_scr = nc.dram_tensor("wmx_scr", [128, 32], F32)

    es = contextlib.ExitStack()
    A = lambda name, shape, dt=F32: alloc_sb(es, nc, name, shape, dt)
    P = lambda name, shape, dt=F32: alloc_ps(es, nc, name, shape, dt)
    w1q = A("w1q", [128, 8 * 4096], BF16)
    w2q = A("w2q", [128, 32 * D], BF16)
    idf = A("idf", [128, 128], F32)
    wst = A("wst", [128, 1024], F32)
    si32 = A("si32", [128, 1024], I32)
    back = A("back", [128, 1024], F32)
    dbuf = A("dbuf", [128, 1024], F32)
    wcm = A("wcm", [128, 4096], BF16)
    xq = A("xqf", [128, 8 * 256], BF16)
    hb = A("hbf", [128, 32 * 256], BF16)
    xs0 = A("xs0", [128, D], F32)
    xxt = A("xxt", [128, D], F32)
    gte = A("gte", [128, D], F32)
    sc = A("sc", [128, 8], F32)
    sfx = A("sfx", [128, 32], F32)
    shs = A("shs", [128, 32], F32)
    wmx = A("wmx", [128, 32], F32)
    wrm = A("wrm", [128, 32], F32)
    row = A("row", [1, 4096], F32)
    r11 = A("r11", [1, 8], F32)
    ones_r = A("ones_r", [1, 128], F32)
    b1s = A("b1s", [128, 32], F32)
    b2bc = A("b2bc", [128, D], F32)
    macc = A("macc", [128, 2], F32)
    ps_a = [P(f"ps_a{i}", [128, 512]) for i in range(2)]
    ps_c = P("ps_c", [128, 1])
    ps_b = P("ps_b", [128, 512])
    ps_tr = P("ps_tr", [128, 128])

    pb = PB(nc)
    op1(pb, "vector", "memset", ones_r[:], 1.0)
    pb.dma(idf[:], ident, engine="sync")
    pb.dma(b1s[:], binf, engine="sync")
    pb.dma(row[0:1, 0:1024], xnmax_all, engine="sync")
    red(pb, r11[:, 0:1], row[0:1, 0:1024], ALU.max)
    ts(pb, r11[:, 0:1], r11[:, 0:1], 1e-8, None, ALU.max)
    ts(pb, r11[:, 1:2], r11[:, 0:1], 1.0 / 32767.0)
    op1(pb, "vector", "reciprocal", r11[:, 2:3], r11[:, 1:2])
    repl_scalar(pb, sc[:, 0:1], r11[:, 2:3], ones_r, ps_c)
    repl_scalar(pb, sc[:, 6:7], r11[:, 1:2], ones_r, ps_c)
    for st4 in range(4):
        sl = slice(st4 * 1024, (st4 + 1) * 1024)
        for ch in range(8):
            pb.dma(wst[:], Win[ch * 128:(ch + 1) * 128, sl], engine="sync")
            act(pb, back[:], wst[:], AF.Abs)
            if ch == 0:
                cp(pb, wcm[:, sl], back[:])
            else:
                def fmx(eng, _o=wcm[:, sl], _b=back[:]):
                    return eng.tensor_tensor(_o, _o, _b, ALU.max)
                pb.add("vector", fmx, reads=["wcm", "back"], writes=["wcm"])
    for tcol in range(32):
        cp(pb, wst[:, 0:128], wcm[:, tcol * 128:(tcol + 1) * 128])
        transpose_op(pb, ps_tr, wst[:, 0:128], idf)
        red(pb, wmx[:, tcol:tcol + 1], ps_tr, ALU.max)
    ts(pb, wmx[:], wmx[:], 1e-8, None, ALU.max)
    ts(pb, sfx[:], wmx[:], 1.0 / 127.0, sc[:, 6:7], ALU.mult, ALU.mult)
    pb.dma(wmx_scr.ap(), wmx[:], engine="sync")
    pb.dma(row[:].rearrange("one (a b) -> one a b", a=32),
           wmx_scr.ap().transpose([1, 0]).unsqueeze(0), engine="sync",
           allow_slow_non_contiguous=True)
    op1(pb, "vector", "reciprocal", row[:], row[:])
    ts(pb, row[:], row[:], 127.0)
    for nh in range(8):
        sl = slice(nh * 512, (nh + 1) * 512)
        mm_acc(pb, ps_b, [(ones_r, row[0:1, sl])])
        cp(pb, wcm[:, sl], ps_b)
    for ch in range(8):
        for st4 in range(4):
            sl = slice(st4 * 1024, (st4 + 1) * 1024)
            ch4sl = sl
            pb.dma(wst[:], Win[ch * 128:(ch + 1) * 128, sl], engine="sync")
            tt(pb, wst[:], wst[:], wcm[:, ch4sl], ALU.mult)
            act(pb, si32[:], wst[:], AF.Copy)
            act(pb, back[:], si32[:], AF.Copy)
            tt(pb, dbuf[:], wst[:], back[:], ALU.subtract)
            ts(pb, wst[:], dbuf[:], 0.5, None, ALU.is_gt)
            tt(pb, back[:], back[:], wst[:], ALU.add)
            ts(pb, wst[:], dbuf[:], -0.5, None, ALU.is_lt)
            tt(pb, w1q[:, ch * 4096 + sl.start: ch * 4096 + sl.stop],
               back[:], wst[:], ALU.subtract)
    for ch in range(32):
        sl2 = slice(ch * 128, (ch + 1) * 128)
        pb.dma(wst[:], Wout[sl2, :], engine="sync")
        red(pb, wrm[:, ch:ch + 1], wst[:], ALU.max, absval=True)
        ts(pb, wrm[:, ch:ch + 1], wrm[:, ch:ch + 1], 1e-8, None, ALU.max)
        op1(pb, "vector", "reciprocal", sc[:, 7:8], wrm[:, ch:ch + 1])
        ts(pb, sc[:, 7:8], sc[:, 7:8], 127.0)
        quant_round(pb, w2q[:, ch * D:(ch + 1) * D], wst[:], sc[:, 7:8],
                    si32, back, dbuf, wst)
    op1(pb, "vector", "memset", macc[:], 0.0)
    NSLAB = SH // 256
    for s in range(NSLAB):
        pb.dma(xq[:].rearrange("p (c t) -> p c t", c=8),
               xnfm[:, :, s * 256:(s + 1) * 256].transpose([1, 0, 2]),
               engine="sync")
        for c8 in range(2):
            sl = slice(c8 * 1024, (c8 + 1) * 1024)
            quant_round(pb, xq[:, sl], xq[:, sl], sc[:, 0:1],
                        si32, back, dbuf, wst)
        for fb in range(32):
            pa = ps_a[fb % 2]
            mm_acc(pb, pa[:, 0:256], [
                (w1q[:, dc * 4096 + fb * 128: dc * 4096 + (fb + 1) * 128],
                 xq[:, dc * 256:(dc + 1) * 256]) for dc in range(8)])
            act(pb, hb[:, fb * 256:(fb + 1) * 256], pa[:, 0:256], AF.Gelu,
                bias=b1s[:, fb:fb + 1], scale=sfx[:, fb:fb + 1])
            red(pb, macc[:, 1:2], hb[:, fb * 256:(fb + 1) * 256],
                ALU.max, absval=True)
            tt(pb, macc[:, 0:1], macc[:, 0:1], macc[:, 1:2], ALU.max)
        pb.dma(h_fm.transpose([1, 0, 2])[:, :, s * 256:(s + 1) * 256],
               hb[:].rearrange("p (c t) -> p c t", c=32), engine="sync")
    part_reduce_via_dram(pb, es, nc, r11[:, 1:2], macc[:, 0:1], "h2red")
    cp(pb, row[0:1, 0:4], r11[:, 1:2].broadcast_to([1, 4]))
    pb.dma(cbar.ap(), row[0:1, 0:4], engine="gdma")
    pb.coll("AllReduce", ALU.max, [cbar.ap()], [cbar_o.ap()])
    pb.dma(r11[:, 1:2], cbar_o.ap()[0:1, 0:1], engine="gdma")
    ts(pb, r11[:, 1:2], r11[:, 1:2], 1e-8, None, ALU.max)
    ts(pb, r11[:, 2:3], r11[:, 1:2], 1.0 / 32767.0)
    op1(pb, "vector", "reciprocal", r11[:, 3:4], r11[:, 2:3])
    repl_scalar(pb, sc[:, 4:5], r11[:, 3:4], ones_r, ps_c)
    repl_scalar(pb, sc[:, 5:6], r11[:, 2:3], ones_r, ps_c)
    ts(pb, shs[:], wrm[:], 1.0 / 127.0, sc[:, 5:6], ALU.mult, ALU.mult)
    pb.dma(row[0:1, 0:1024], bor, engine="sync")
    for half in range(2):
        mm_acc(pb, ps_a[0], [(ones_r, row[0:1, half * 512:(half + 1) * 512])])
        cp(pb, b2bc[:, half * 512:(half + 1) * 512], ps_a[0])
    for s in range(NSLAB):
        pb.dma(hb[:].rearrange("p (c t) -> p c t", c=32),
               h_fm.transpose([1, 0, 2])[:, :, s * 256:(s + 1) * 256],
               engine="sync")
        for st8 in range(8):
            sl = slice(st8 * 1024, (st8 + 1) * 1024)
            quant_round(pb, hb[:, sl], hb[:, sl], sc[:, 4:5],
                        si32, back, dbuf, wst)
        for fb in range(32):
            def fsc(eng, _o=hb[:, fb * 256:(fb + 1) * 256], _s=shs[:, fb:fb + 1]):
                return eng.tensor_single_scalar(_o, _o, _s, ALU.mult)
            pb.add("vector", fsc, reads=["hbf", "shs"], writes=["hbf"])
        for tb in range(2):
            tok = slice(s * 256 + tb * 128, s * 256 + (tb + 1) * 128)
            pb.dma(xs0[:], s2[tok, :], engine="sync")
            pb.dma(xxt[:], x[tok, :], engine="sync")
            for half in range(2):
                pa = ps_a[half]
                mm_acc(pb, pa, [
                    (hb[:, fb * 256 + tb * 128: fb * 256 + tb * 128 + 128],
                     w2q[:, fb * D + half * 512: fb * D + (half + 1) * 512])
                    for fb in range(32)])
                hs = slice(half * 512, (half + 1) * 512)
                tt(pb, gte[:, hs], pa, b2bc[:, hs], ALU.add)
            act(pb, gte[:], gte[:], AF.Silu)
            tt(pb, gte[:], xs0[:], gte[:], ALU.mult)
            tt(pb, gte[:], gte[:], xxt[:], ALU.add)
            pb.dma(o_x3[tok, :], gte[:], engine="sync")
    pb.emit()
    es.close()
    return nc


def build_L3r():
    nc = bass.Bass("TRN2", target_bir_lowering=False, debug=False)
    gens = nc.dram_tensor("gens", [3, D, D], F32, kind="ExternalInput").ap()
    islab = nc.dram_tensor("islab", [D, 128], F32, kind="ExternalInput").ap()
    o_R = nc.dram_tensor("Rslab", [3, D, 128], F32, kind="ExternalOutput").ap()
    NTAY = 18

    es = contextlib.ExitStack()
    A = lambda name, shape, dt=F32: alloc_sb(es, nc, name, shape, dt)
    P = lambda name, shape, dt=F32: alloc_ps(es, nc, name, shape, dt)
    Ahi = A("Ahi", [128, 8 * 1024], BF16)
    Alo = A("Alo", [128, 8 * 1024], BF16)
    af = A("af", [128, 1024], F32)
    negf = A("negf", [128, 1024], F32)
    hif = A("hif", [128, 1024], F32)
    isl = A("isl", [128, 8 * 128], F32)
    Pa = A("Pa", [128, 8 * 128], F32)
    Pb = A("Pb", [128, 8 * 128], F32)
    Phi = A("Phi", [128, 8 * 128], BF16)
    Plo = A("Plo", [128, 8 * 128], BF16)
    phif = A("phif", [128, 8 * 128], F32)
    ps_q = [P(f"ps_q{i}", [128, 128]) for i in range(2)]

    pb = PB(nc)
    for jc in range(8):
        pb.dma(isl[:, jc * 128:(jc + 1) * 128],
               islab[jc * 128:(jc + 1) * 128, :], engine="sync")
    for g in range(3):
        for jc in range(8):
            pb.dma(af[:], gens[g, jc * 128:(jc + 1) * 128, :], engine="sync")
            ts(pb, negf[:], af[:], -1.0)
            cp(pb, Ahi[:, jc * 1024:(jc + 1) * 1024], negf[:])
            cp(pb, hif[:], Ahi[:, jc * 1024:(jc + 1) * 1024])
            tt(pb, af[:], negf[:], hif[:], ALU.subtract)
            cp(pb, Alo[:, jc * 1024:(jc + 1) * 1024], af[:])
        cur, nxt = Pa, Pb
        cp(pb, cur[:], isl[:])
        for k in range(NTAY, 0, -1):
            cp(pb, Phi[:], cur[:])
            cp(pb, phif[:], Phi[:])
            tt(pb, phif[:], cur[:], phif[:], ALU.subtract)
            cp(pb, Plo[:], phif[:])
            for ib in range(8):
                pairs = []
                for jc in range(8):
                    lh = Ahi[:, jc * 1024 + ib * 128: jc * 1024 + (ib + 1) * 128]
                    ll = Alo[:, jc * 1024 + ib * 128: jc * 1024 + (ib + 1) * 128]
                    ph = Phi[:, jc * 128:(jc + 1) * 128]
                    pl = Plo[:, jc * 128:(jc + 1) * 128]
                    pairs += [(lh, ph), (lh, pl), (ll, ph)]
                pq = ps_q[ib % 2]
                mm_acc(pb, pq, pairs)
                stt(pb, nxt[:, ib * 128:(ib + 1) * 128], pq[:], 1.0 / k,
                    isl[:, ib * 128:(ib + 1) * 128], ALU.mult, ALU.add)
            cur, nxt = nxt, cur
        for jc in range(8):
            pb.dma(o_R[g, jc * 128:(jc + 1) * 128, :],
                   cur[:, jc * 128:(jc + 1) * 128], engine="sync")
    pb.emit()
    es.close()
    return nc


def build_L3a():
    nc = bass.Bass("TRN2", target_bir_lowering=False, debug=False)
    x = nc.dram_tensor("x", [SH, D], F32, kind="ExternalInput").ap()
    wqkvT = nc.dram_tensor("wqkvT", [D, 3072], F32, kind="ExternalInput").ap()
    bqkv = nc.dram_tensor("bqkv", [1, 3072], F32, kind="ExternalInput").ap()
    projT = nc.dram_tensor("projT", [64, 256], F32, kind="ExternalInput").ap()
    ident = nc.dram_tensor("ident", [128, 128], F32, kind="ExternalInput").ap()
    o_xn = nc.dram_tensor("xn4", [SH, D], F32, kind="ExternalOutput").ap()
    o_q = nc.dram_tensor("q_t", [SH, D], F32, kind="ExternalOutput").ap()
    o_k = nc.dram_tensor("k_t", [SH, D], F32, kind="ExternalOutput").ap()
    o_v = nc.dram_tensor("v_t", [SH, D], F32, kind="ExternalOutput").ap()
    o_kmax = nc.dram_tensor("kmax", [1, 1], F32, kind="ExternalOutput").ap()
    SCL = float(64.0 ** -0.25)

    es = contextlib.ExitStack()
    A = lambda name, shape, dt=F32: alloc_sb(es, nc, name, shape, dt)
    P = lambda name, shape, dt=F32: alloc_ps(es, nc, name, shape, dt)
    wb = A("wb", [128, 8 * 3072], BF16)
    bqb = A("bqb", [128, 3072], F32)
    projTb = A("projTb", [64, 256], BF16)
    prf = A("prf", [64, 256], F32)
    idb = A("idb", [128, 128], BF16)
    ones_r = A("ones_r", [1, 128], F32)
    xt = A("xt", [128, D], F32)
    xnf = A("xnf", [128, D], F32)
    xnb = A("xnb", [128, D], BF16)
    xnfm = A("xnfm", [128, 8 * 128], BF16)
    qkvf = A("qkvf", [128, 3072], F32)
    khb = A("khb", [128, 64], BF16)
    kfm = A("kfm", [64, 128], BF16)
    sq = A("sq", [128, 1024], F32)
    st = A("st", [128, 8], F32)
    km_acc = A("km_acc", [128, 1], F32)
    km11 = A("km11", [1, 1], F32)
    ps_tr = P("ps_tr", [128, 128], BF16)
    ps_p = [P(f"ps_p{i}", [128, 512]) for i in range(2)]
    ps_dd = P("ps_dd", [128, 256])
    ps_b = P("ps_b", [128, 512])

    pb = PB(nc)
    cpool = ConstPool(pb, es, nc)
    eps = cpool.col(1e-5)
    op1(pb, "vector", "memset", ones_r[:], 1.0)
    op1(pb, "vector", "memset", km_acc[:], -1e30)
    pb.dma(idb[:], ident, engine="gdma")
    for jc in range(8):
        pb.dma(wb[:, jc * 3072:(jc + 1) * 3072],
               wqkvT[jc * 128:(jc + 1) * 128, :], engine="gdma")
    pb.dma(prf[:], projT, engine="sync")
    cp(pb, projTb[:], prf[:])
    brow = A("brow", [1, 3072], F32)
    pb.dma(brow[:], bqkv, engine="sync")
    for part in range(6):
        rowbc(pb, bqb[:, part * 512:(part + 1) * 512],
              brow[0:1, part * 512:(part + 1) * 512], ones_r, ps_b)

    for t in range(SH // 128):
        tsl = slice(t * 128, (t + 1) * 128)
        pb.dma(xt[:], x[tsl, :], engine="sync")
        layernorm128(pb, xnf[:], xt[:], sq, st, eps)
        pb.dma(o_xn[tsl, :], xnf[:], engine="sync")
        cp(pb, xnb[:], xnf[:])
        for jc in range(8):
            transpose_op(pb, ps_tr, xnb[:, jc * 128:(jc + 1) * 128], idb)
            cp(pb, xnfm[:, jc * 128:(jc + 1) * 128], ps_tr)
        for part in range(6):
            pp = ps_p[part % 2]
            mm_acc(pb, pp, [
                (xnfm[:, jc * 128:(jc + 1) * 128],
                 wb[:, jc * 3072 + part * 512: jc * 3072 + (part + 1) * 512])
                for jc in range(8)])
            tt(pb, qkvf[:, part * 512:(part + 1) * 512], pp[:],
               bqb[:, part * 512:(part + 1) * 512], ALU.add)
        ts(pb, qkvf[:, 0:2048], qkvf[:, 0:2048], SCL)
        pb.dma(o_q[tsl, :], qkvf[:, 0:1024], engine="sync")
        pb.dma(o_k[tsl, :], qkvf[:, 1024:2048], engine="sync")
        pb.dma(o_v[tsl, :], qkvf[:, 2048:3072], engine="sync")
        for h in range(16):
            cp(pb, khb[:], qkvf[:, 1024 + h * 64: 1024 + (h + 1) * 64])
            transpose_op(pb, ps_tr[0:64, 0:128], khb[:], idb)
            cp(pb, kfm[:], ps_tr[0:64, 0:128])
            mm_acc(pb, ps_dd, [(kfm[:], projTb[:])])
            red(pb, st[:, 0:1], ps_dd[:], ALU.max)
            tt(pb, km_acc[:], km_acc[:], st[:, 0:1], ALU.max)
    part_reduce_via_dram(pb, es, nc, km11[:], km_acc[:], "kmscr", ALU.max)
    pb.dma(o_kmax, km11[:], engine="sync")
    pb.emit()
    es.close()
    return nc


def build_L3b():
    nc = bass.Bass("TRN2", target_bir_lowering=False, debug=False)
    qp = nc.dram_tensor("qp", [4, T, 64], F32, kind="ExternalInput").ap()
    kp = nc.dram_tensor("kp", [4, T, 64], F32, kind="ExternalInput").ap()
    vp = nc.dram_tensor("vp", [4, T, 64], F32, kind="ExternalInput").ap()
    kmax_all = nc.dram_tensor("kmax_all", [1, 8], F32, kind="ExternalInput").ap()
    projT = nc.dram_tensor("projT", [64, 256], F32, kind="ExternalInput").ap()
    ident = nc.dram_tensor("ident", [128, 128], F32, kind="ExternalInput").ap()
    o_o = nc.dram_tensor("o", [4, T, 64], F32, kind="ExternalOutput").ap()
    kss = nc.dram_tensor("kss", [256], F32)
    RAT, KEPS = 1.0 / 16.0, 1e-4

    es = contextlib.ExitStack()
    A = lambda name, shape, dt=F32: alloc_sb(es, nc, name, shape, dt)
    P = lambda name, shape, dt=F32: alloc_ps(es, nc, name, shape, dt)
    projTb = A("projTb", [64, 256], BF16)
    prf = A("prf", [64, 256], F32)
    idb = A("idb", [128, 128], BF16)
    ones_r = A("ones_r", [1, 128], F32)
    onesb = A("onesb", [128, 1], BF16)
    onesf = A("onesf", [128, 1], F32)
    row8 = A("row8", [1, 8], F32)
    k11 = A("k11", [1, 1], F32)
    kst = A("kst", [128, 1], F32)
    kt_ = A("kt_", [128, 64], F32)
    vt_ = A("vt_", [128, 64], F32)
    vb = A("vb", [128, 64], BF16)
    khb = A("khb", [128, 64], BF16)
    kfm = A("kfm", [64, 128], BF16)
    kf_f = A("kf_f", [128, 256], F32)
    kfb = A("kfb", [128, 256], BF16)
    qfm = A("qfm", [128, 2 * 128], BF16)
    ctx_sb = A("ctx_sb", [128, 128], F32)
    ctxb = A("ctxb", [128, 128], BF16)
    ks_sb = A("ks_sb", [128, 2], F32)
    ksrow = A("ksrow", [1, 256], F32)
    ksb = A("ksb", [128, 256], F32)
    ot = A("ot", [128, 64], F32)
    sq = A("sq", [128, 256], F32)
    st = A("st", [128, 8], F32)
    ps_tr = P("ps_tr", [128, 128], BF16)
    ps_dd = P("ps_dd", [128, 256])
    ps_c = P("ps_c", [128, 64])
    ps_k = P("ps_k", [128, 1])
    ps_o = P("ps_o", [128, 64])
    ps_b = P("ps_b", [128, 256])

    pb = PB(nc)
    op1(pb, "vector", "memset", ones_r[:], 1.0)
    op1(pb, "vector", "memset", onesf[:], 1.0)
    cp(pb, onesb[:], onesf[:])
    pb.dma(idb[:], ident, engine="gdma")
    pb.dma(prf[:], projT, engine="sync")
    cp(pb, projTb[:], prf[:])
    pb.dma(row8[:], kmax_all, engine="sync")
    red(pb, k11[:], row8[:], ALU.max)
    repl_scalar(pb, kst[:], k11[:], ones_r, ps_k)

    NT = T // 128
    for p in range(4):
        op1(pb, "vector", "memset", ctx_sb[:], 0.0)
        op1(pb, "vector", "memset", ks_sb[:], 0.0)
        for i in range(NT):
            tsl = slice(i * 128, (i + 1) * 128)
            pb.dma(kt_[:], kp[p, tsl, :], engine="sync")
            pb.dma(vt_[:], vp[p, tsl, :], engine="sync")
            act(pb, sq[:, 0:64], kt_[:], AF.Square)
            red(pb, st[:, 0:1], sq[:, 0:64], ALU.add)
            stt(pb, st[:, 1:2], st[:, 0:1], -0.5, kst[:], ALU.mult,
                ALU.subtract)
            cp(pb, khb[:], kt_[:])
            transpose_op(pb, ps_tr[0:64, 0:128], khb[:], idb)
            cp(pb, kfm[:], ps_tr[0:64, 0:128])
            mm_acc(pb, ps_dd, [(kfm[:], projTb[:])])
            act(pb, kf_f[:], ps_dd[:], AF.Exp, bias=st[:, 1:2])
            ts(pb, kf_f[:], kf_f[:], KEPS, RAT, ALU.add, ALU.mult)
            cp(pb, kfb[:], kf_f[:])
            cp(pb, vb[:], vt_[:])
            for hf in range(2):
                lh = kfb[:, hf * 128:(hf + 1) * 128]
                mm_acc(pb, ps_c, [(lh, vb[:])])
                tt(pb, ctx_sb[:, hf * 64:(hf + 1) * 64],
                   ctx_sb[:, hf * 64:(hf + 1) * 64], ps_c[:], ALU.add)
                mm_acc(pb, ps_k, [(lh, onesb[:])])
                tt(pb, ks_sb[:, hf:hf + 1], ks_sb[:, hf:hf + 1], ps_k[:],
                   ALU.add)
        cp(pb, ctxb[:], ctx_sb[:])
        pb.dma(kss.ap()[0:128], ks_sb[:, 0:1], engine="sync")
        pb.dma(kss.ap()[128:256], ks_sb[:, 1:2], engine="sync")
        pb.dma(ksrow[:], kss.ap().rearrange("(a b) -> a b", a=1),
               engine="sync")
        rowbc(pb, ksb[:], ksrow[:], ones_r, ps_b)
        for i in range(NT):
            tsl = slice(i * 128, (i + 1) * 128)
            pb.dma(kt_[:], qp[p, tsl, :], engine="sync")
            act(pb, sq[:, 0:64], kt_[:], AF.Square)
            red(pb, st[:, 0:1], sq[:, 0:64], ALU.add)
            cp(pb, khb[:], kt_[:])
            transpose_op(pb, ps_tr[0:64, 0:128], khb[:], idb)
            cp(pb, kfm[:], ps_tr[0:64, 0:128])
            mm_acc(pb, ps_dd, [(kfm[:], projTb[:])])
            red(pb, st[:, 2:3], ps_dd[:], ALU.max)
            stt(pb, st[:, 1:2], st[:, 0:1], -0.5, st[:, 2:3], ALU.mult,
                ALU.subtract)
            act(pb, kf_f[:], ps_dd[:], AF.Exp, bias=st[:, 1:2])
            ts(pb, kf_f[:], kf_f[:], KEPS, RAT, ALU.add, ALU.mult)
            tt(pb, sq[:], kf_f[:], ksb[:], ALU.mult)
            red(pb, st[:, 3:4], sq[:], ALU.add)
            op1(pb, "vector", "reciprocal", st[:, 4:5], st[:, 3:4])
            cp(pb, kfb[:], kf_f[:])
            for hf in range(2):
                transpose_op(pb, ps_tr, kfb[:, hf * 128:(hf + 1) * 128], idb)
                cp(pb, qfm[:, hf * 128:(hf + 1) * 128], ps_tr)
            mm_acc(pb, ps_o, [
                (qfm[:, hf * 128:(hf + 1) * 128],
                 ctxb[:, hf * 64:(hf + 1) * 64]) for hf in range(2)])
            ts(pb, ot[:], ps_o[:], st[:, 4:5])
            pb.dma(o_o[p, tsl, :], ot[:], engine="sync")
    pb.emit()
    es.close()
    return nc


def build_L3c():
    nc = bass.Bass("TRN2", target_bir_lowering=False, debug=False)
    o_as = nc.dram_tensor("o_as", [SH, D], F32, kind="ExternalInput").ap()
    xn4 = nc.dram_tensor("xn4", [SH, D], F32, kind="ExternalInput").ap()
    x3 = nc.dram_tensor("x3", [SH, D], F32, kind="ExternalInput").ap()
    woT = nc.dram_tensor("woT", [D, D], F32, kind="ExternalInput").ap()
    bo = nc.dram_tensor("bo", [1, D], F32, kind="ExternalInput").ap()
    lng = nc.dram_tensor("lng", [1, D], F32, kind="ExternalInput").ap()
    lnb = nc.dram_tensor("lnb", [1, D], F32, kind="ExternalInput").ap()
    abw = nc.dram_tensor("abw", [D, D], F32, kind="ExternalInput").ap()
    abb = nc.dram_tensor("abb", [1, D], F32, kind="ExternalInput").ap()
    mgw = nc.dram_tensor("mgw", [D, 8], F32, kind="ExternalInput").ap()
    mgb = nc.dram_tensor("mgb", [1, 8], F32, kind="ExternalInput").ap()
    ew1 = nc.dram_tensor("ew1", [8, D, 2048], F32, kind="ExternalInput").ap()
    eb1 = nc.dram_tensor("eb1", [8, 2048], F32, kind="ExternalInput").ap()
    ew2 = nc.dram_tensor("ew2", [8, 2048, D], F32, kind="ExternalInput").ap()
    eb2 = nc.dram_tensor("eb2", [8, D], F32, kind="ExternalInput").ap()
    R3s = nc.dram_tensor("R3s", [3, D, D], F32, kind="ExternalInput").ap()
    gcw = nc.dram_tensor("gcw", [1, 2048], F32, kind="ExternalInput").ap()
    gcb = nc.dram_tensor("gcb", [1, 1], F32, kind="ExternalInput").ap()
    ident = nc.dram_tensor("ident", [128, 128], F32, kind="ExternalInput").ap()
    o_x4 = nc.dram_tensor("x4", [SH, D], F32, kind="ExternalOutput").ap()
    xni_scr = nc.dram_tensor("xni_scr", [SH, D], BF16)
    abel_scr = nc.dram_tensor("abel_scr", [SH, D], BF16)
    s3_scr = nc.dram_tensor("s3_scr", [SH, D], F32)

    es = contextlib.ExitStack()
    A = lambda name, shape, dt=F32: alloc_sb(es, nc, name, shape, dt)
    P = lambda name, shape, dt=F32: alloc_ps(es, nc, name, shape, dt)
    ew1b = A("ew1b", [128, 8 * 2048], BF16)
    ew2b = A("ew2b", [128, 16 * 1024], BF16)
    macc = A("macc", [128, 16 * 1024], F32)
    mgwb = A("mgwb", [128, 8 * 8], BF16)
    idb = A("idb", [128, 128], BF16)
    ones_r = A("ones_r", [1, 128], F32)
    rowbuf = A("rowbuf", [1, 2048], F32)
    b1b = A("b1b", [128, 2048], F32)
    b2b = A("b2b", [128, 1024], F32)
    bob = A("bob", [128, 1024], F32)
    gab = A("gab", [128, 1024], F32)
    gbb = A("gbb", [128, 1024], F32)
    gcbc = A("gcbc", [128, 1], F32)
    wdall = A("wdall", [128, 32 * 8], F32)
    xnifm = A("xnifm", [128, 8 * 128], BF16)
    xnib = A("xnib", [128, 1024], BF16)
    hb = A("hb", [128, 2048], BF16)
    hfm = A("hfm", [128, 16 * 128], BF16)
    w0 = A("w0", [128, 1024], F32)
    w1 = A("w1", [128, 1024], F32)
    w2 = A("w2", [128, 1024], F32)
    sq = A("sq", [128, 1024], F32)
    st = A("st", [128, 8], F32)
    lg = A("lg", [128, 8], F32)
    m8 = A("m8", [128, 8], F32)
    ps_a = [P(f"ps_a{i}", [128, 512]) for i in range(2)]
    ps_tr = P("ps_tr", [128, 128], BF16)
    ps_8 = P("ps_8", [128, 8])
    ps_b = P("ps_b", [128, 512])

    pb = PB(nc)
    cpool = ConstPool(pb, es, nc)
    eps = cpool.col(1e-5)
    op1(pb, "vector", "memset", ones_r[:], 1.0)
    pb.dma(idb[:], ident, engine="gdma")
    for jc in range(8):
        pb.dma(ew1b[:, jc * 1024:(jc + 1) * 1024],
               woT[jc * 128:(jc + 1) * 128, :], engine="gdma")
        pb.dma(ew2b[:, jc * 1024:(jc + 1) * 1024],
               abw[jc * 128:(jc + 1) * 128, :], engine="gdma")
        pb.dma(mgwb[:, jc * 8:(jc + 1) * 8],
               mgw[jc * 128:(jc + 1) * 128, :], engine="gdma")
    def bc_row(dst, src_row, width):
        pb.dma(rowbuf[0:1, 0:width], src_row, engine="sync")
        for part in range(0, width, 512):
            w = min(512, width - part)
            rowbc(pb, dst[:, part:part + w], rowbuf[0:1, part:part + w],
                  ones_r, ps_b[:, 0:w])
    bc_row(bob, bo, 1024)
    bc_row(gab, lng, 1024)
    bc_row(gbb, lnb, 1024)
    bc_row(b2b, abb, 1024)
    bc_row(b1b, mgb, 8)
    pb.dma(rowbuf[0:1, 0:1], gcb, engine="sync")
    repl_scalar(pb, gcbc[:], rowbuf[0:1, 0:1], ones_r, ps_8[:, 0:1])

    for t in range(32):
        tsl = slice(t * 128, (t + 1) * 128)
        pb.dma(w0[:], xn4[tsl, :], engine="sync")
        layernorm128(pb, w1[:], w0[:], sq, st, eps)
        tt(pb, w1[:], w1[:], gab[:], ALU.mult)
        tt(pb, w1[:], w1[:], gbb[:], ALU.add)
        cp(pb, xnib[:], w1[:])
        pb.dma(xni_scr.ap()[tsl, :], xnib[:], engine="sync")
        for jc in range(8):
            transpose_op(pb, ps_tr, xnib[:, jc * 128:(jc + 1) * 128], idb)
            cp(pb, xnifm[:, jc * 128:(jc + 1) * 128], ps_tr)
        for half in range(2):
            pa = ps_a[half]
            mm_acc(pb, pa, [
                (xnifm[:, jc * 128:(jc + 1) * 128],
                 ew2b[:, jc * 1024 + half * 512: jc * 1024 + half * 512 + 512])
                for jc in range(8)])
            tt(pb, sq[:, half * 512:(half + 1) * 512], pa[:],
               b2b[:, half * 512:(half + 1) * 512], ALU.add)
        act(pb, w2[:, 0:512], sq[:, 0:512], AF.Gelu)
        act(pb, w2[:, 512:1024], sq[:, 512:1024], AF.Gelu)
        cp(pb, xnib[:], w2[:])
        pb.dma(abel_scr.ap()[tsl, :], xnib[:], engine="sync")
        mm_acc(pb, ps_8, [
            (xnifm[:, jc * 128:(jc + 1) * 128], mgwb[:, jc * 8:(jc + 1) * 8])
            for jc in range(8)])
        tt(pb, lg[:], ps_8[:], b1b[:, 0:8], ALU.add)
        red(pb, st[:, 0:1], lg[:], ALU.max)
        ts(pb, m8[:], lg[:], st[:, 0:1], None, ALU.is_equal)
        ts(pb, sq[:, 0:8], m8[:], -1e9)
        tt(pb, sq[:, 0:8], lg[:], sq[:, 0:8], ALU.add)
        red(pb, st[:, 1:2], sq[:, 0:8], ALU.max)
        ts(pb, sq[:, 8:16], sq[:, 0:8], st[:, 1:2], None, ALU.is_equal)
        tt(pb, st[:, 2:3], st[:, 0:1], st[:, 1:2], ALU.subtract)
        act(pb, st[:, 3:4], st[:, 2:3], AF.Sigmoid)
        ts(pb, st[:, 4:5], st[:, 3:4], -1.0, 1.0, ALU.mult, ALU.add)
        ts(pb, m8[:], m8[:], st[:, 3:4])
        stt(pb, wdall[:, t * 8:(t + 1) * 8], sq[:, 8:16], st[:, 4:5], m8[:],
            ALU.mult, ALU.add)
        pb.dma(w0[:], o_as[tsl, :], engine="sync")
        cp(pb, xnib[:], w0[:])
        for jc in range(8):
            transpose_op(pb, ps_tr, xnib[:, jc * 128:(jc + 1) * 128], idb)
            cp(pb, xnifm[:, jc * 128:(jc + 1) * 128], ps_tr)
        for half in range(2):
            pa = ps_a[half]
            mm_acc(pb, pa, [
                (xnifm[:, jc * 128:(jc + 1) * 128],
                 ew1b[:, jc * 1024 + half * 512: jc * 1024 + half * 512 + 512])
                for jc in range(8)])
            tt(pb, w1[:, half * 512:(half + 1) * 512], pa[:],
               bob[:, half * 512:(half + 1) * 512], ALU.add)
        pb.dma(s3_scr.ap()[tsl, :], w1[:], engine="sync")

    bc_row(gab, gcw[0:1, 0:1024], 1024)
    bc_row(gbb, gcw[0:1, 1024:2048], 1024)

    for sl in range(2):
        t0 = sl * 16
        for e in range(8):
            for jc in range(8):
                pb.dma(ew1b[:, jc * 2048:(jc + 1) * 2048],
                       ew1[e, jc * 128:(jc + 1) * 128, :], engine="gdma")
            for hc in range(16):
                pb.dma(ew2b[:, hc * 1024:(hc + 1) * 1024],
                       ew2[e, hc * 128:(hc + 1) * 128, :], engine="gdma")
            bc_row(b1b, eb1[e:e + 1, :], 2048)
            bc_row(b2b, eb2[e:e + 1, :], 1024)
            for tl in range(16):
                t = t0 + tl
                tsl = slice(t * 128, (t + 1) * 128)
                pb.dma(xnib[:], xni_scr.ap()[tsl, :], engine="sync")
                for jc in range(8):
                    transpose_op(pb, ps_tr, xnib[:, jc * 128:(jc + 1) * 128],
                                 idb)
                    cp(pb, xnifm[:, jc * 128:(jc + 1) * 128], ps_tr)
                for q in range(4):
                    pa = ps_a[q % 2]
                    mm_acc(pb, pa, [
                        (xnifm[:, jc * 128:(jc + 1) * 128],
                         ew1b[:, jc * 2048 + q * 512: jc * 2048 + q * 512 + 512])
                        for jc in range(8)])
                    tt(pb, sq[:, 0:512], pa[:], b1b[:, q * 512:(q + 1) * 512],
                       ALU.add)
                    act(pb, hb[:, q * 512:(q + 1) * 512], sq[:, 0:512], AF.Gelu)
                for hc in range(16):
                    transpose_op(pb, ps_tr, hb[:, hc * 128:(hc + 1) * 128], idb)
                    cp(pb, hfm[:, hc * 128:(hc + 1) * 128], ps_tr)
                wdc = wdall[:, t * 8 + e: t * 8 + e + 1]
                for half in range(2):
                    pa = ps_a[half]
                    mm_acc(pb, pa, [
                        (hfm[:, hc * 128:(hc + 1) * 128],
                         ew2b[:, hc * 1024 + half * 512: hc * 1024 + half * 512 + 512])
                        for hc in range(16)])
                    msl = macc[:, tl * 1024 + half * 512: tl * 1024 + half * 512 + 512]
                    tt(pb, sq[:, 0:512], pa[:], b2b[:, half * 512:(half + 1) * 512],
                       ALU.add)
                    if e == 0:
                        ts(pb, msl, sq[:, 0:512], wdc)
                    else:
                        stt(pb, msl, sq[:, 0:512], wdc, msl, ALU.mult, ALU.add)
        for r in range(3):
            for jc in range(8):
                pb.dma(ew1b[:, jc * 1024:(jc + 1) * 1024],
                       R3s[r, jc * 128:(jc + 1) * 128, :], engine="gdma")
            for tl in range(16):
                mt = macc[:, tl * 1024:(tl + 1) * 1024]
                cp(pb, xnib[:], mt)
                for jc in range(8):
                    transpose_op(pb, ps_tr, xnib[:, jc * 128:(jc + 1) * 128],
                                 idb)
                    cp(pb, xnifm[:, jc * 128:(jc + 1) * 128], ps_tr)
                for half in range(2):
                    pa = ps_a[half]
                    mm_acc(pb, pa, [
                        (xnifm[:, jc * 128:(jc + 1) * 128],
                         ew1b[:, jc * 1024 + half * 512: jc * 1024 + half * 512 + 512])
                        for jc in range(8)])
                    cp(pb, macc[:, tl * 1024 + half * 512: tl * 1024 + half * 512 + 512],
                       pa[:])
        for tl in range(16):
            t = t0 + tl
            tsl = slice(t * 128, (t + 1) * 128)
            mt = macc[:, tl * 1024:(tl + 1) * 1024]
            pb.dma(xnib[:], abel_scr.ap()[tsl, :], engine="sync")
            cp(pb, w0[:], xnib[:])
            tt(pb, sq[:], w0[:], gab[:], ALU.mult)
            red(pb, st[:, 0:1], sq[:], ALU.add)
            tt(pb, sq[:], mt, gbb[:], ALU.mult)
            red(pb, st[:, 1:2], sq[:], ALU.add)
            tt(pb, st[:, 2:3], st[:, 0:1], st[:, 1:2], ALU.add)
            act(pb, st[:, 3:4], st[:, 2:3], AF.Sigmoid, bias=gcbc[:])
            tt(pb, w1[:], mt, w0[:], ALU.subtract)
            stt(pb, w1[:], w1[:], st[:, 3:4], w0[:], ALU.mult, ALU.add)
            pb.dma(w0[:], xn4[tsl, :], engine="sync")
            tt(pb, w1[:], w1[:], w0[:], ALU.add)
            act(pb, w1[:], w1[:], AF.Silu)
            pb.dma(w0[:], s3_scr.ap()[tsl, :], engine="sync")
            tt(pb, w1[:], w1[:], w0[:], ALU.mult)
            pb.dma(w0[:], x3[tsl, :], engine="sync")
            tt(pb, w1[:], w1[:], w0[:], ALU.add)
            pb.dma(o_x4[tsl, :], w1[:], engine="sync")
    pb.emit()
    es.close()
    return nc


def kernel(**inp):
    f = lambda n: np.ascontiguousarray(np.asarray(inp[n]).astype(np.float32))
    cores = list(range(8))
    ident = np.eye(128, dtype=np.float32)

    x = f("L0_attn_x").reshape(-1, D)

    w_qkv, w_out = f("L0_attn_w_qkv"), f("L0_attn_w_out")
    fc = f("L0_attn_fc_code")[0]
    w1, b1 = f("L0_ffn_w1"), f("L0_ffn_b1")
    w2, b2 = f("L0_ffn_w2"), f("L0_ffn_b2")
    wqkT = np.ascontiguousarray(w_qkv.T)
    codeT = np.ascontiguousarray(fc.transpose(2, 0, 1).reshape(64, 256))
    woT = np.ascontiguousarray(w_out.T)
    triu = np.triu(np.ones((128, 128), np.float32))
    w1T = np.ascontiguousarray(w1.T)
    w2T = np.ascontiguousarray(w2.T)
    b1f = np.ascontiguousarray(b1.reshape(32, 128).T)
    b2r = np.ascontiguousarray(b2.reshape(1, 1024))

    r_a = run_bass_kernel_spmd(build_L0a(), [
        {"x": np.ascontiguousarray(x[c*SH:(c+1)*SH]), "wqkT": wqkT,
         "codeT": codeT, "ident": ident} for c in cores],
        core_ids=cores).results
    kmax_by_batch = [np.stack([np.asarray(r_a[c]["kmax"])
                               for c in range(b*4, b*4+4)]) for b in range(2)]
    r_b = run_bass_kernel_spmd(build_L0b(), [
        {"kc": np.asarray(r_a[c]["kc"]), "v": np.asarray(r_a[c]["v"]),
         "kmax_all": kmax_by_batch[c//4], "ident": ident} for c in cores],
        core_ids=cores).results
    im = []
    for c in cores:
        b = c // 4
        preds = [np.asarray(r_b[p]["totals"]) for p in range(b*4, c)]
        while len(preds) < 3:
            preds.append(np.zeros((16, 16, 65), np.float32))
        im.append({"qc": np.asarray(r_a[c]["qc"]), "kc": np.asarray(r_a[c]["kc"]),
                   "v": np.asarray(r_a[c]["v"]),
                   "kmax_g": np.asarray(r_b[c]["kmax_g"]),
                   "ptot": np.stack(preds[:3]), "woT": woT, "ident": ident,
                   "triu": triu})
    r_c = run_bass_kernel_spmd(build_L0c(), im, core_ids=cores).results
    xnmax_all = np.concatenate([np.asarray(r_a[c]["xnmax"]).ravel()
                                for c in cores]).reshape(1, 1024).astype(np.float32)
    r_f = run_bass_kernel_spmd(build_L0f(), [
        {"xnfm": np.asarray(r_a[c]["xnfm"]), "xnmax_all": xnmax_all,
         "w1T": w1T, "b1f": b1f, "w2T": w2T, "b2r": b2r,
         "s0": np.asarray(r_c[c]["s0"]),
         "x": np.ascontiguousarray(x[c*SH:(c+1)*SH])} for c in cores],
        core_ids=cores).results
    x1 = np.concatenate([np.asarray(r_f[c]["x1"]) for c in cores], axis=0)

    inpw = f("L1_attn_in_proj_w")
    convw, convb = f("L1_attn_conv_w"), f("L1_attn_conv_b")
    xpw = f("L1_attn_x_proj_w")
    dtwF, dtbF = f("L1_attn_dt_w"), f("L1_attn_dt_b")
    alogF, dparF = f("L1_attn_A_log"), f("L1_attn_D_param")
    opw = f("L1_attn_out_proj_w")
    sim, gatesF = f("L1_ffn_sim_matrix"), f("L1_ffn_gates")
    qpF, kpF, vpF = f("L1_ffn_q_proj"), f("L1_ffn_k_proj"), f("L1_ffn_v_proj")
    opF = f("L1_ffn_o_proj")

    r_a = run_bass_kernel_spmd(build_L1a(), [
        {"x": np.ascontiguousarray(x1[c*SH:(c+1)*SH]), "inpw": inpw,
         "sim": sim, "gates": gatesF.reshape(1, 16), "ident": ident}
        for c in cores], core_ids=cores).results
    r_a2 = run_bass_kernel_spmd(build_L1a2(), [
        {"xnfm": np.asarray(r_a[c]["xnfm"]), "w_row": np.asarray(r_a[c]["w_row"]),
         "qp": qpF, "kp": kpF, "vp": vpF, "ident": ident} for c in cores],
        core_ids=cores).results
    convw16 = np.ascontiguousarray(convw[:, 0, :].reshape(16, 128, 4))
    convb16 = np.ascontiguousarray(convb.reshape(16, 128, 1))
    ims = []
    for c in cores:
        if c % 4 == 0:
            halo = np.zeros((16, 128, 3), np.float32)
        else:
            halo = np.asarray(r_a[c - 1]["xin"])[:, :, -3:].astype(np.float32)
        ims.append({"xin": np.asarray(r_a[c]["xin"]), "halo": halo,
                    "convw": convw16, "convb": convb16, "xpw": xpw})
    r_b = run_bass_kernel_spmd(build_L1b(), ims, core_ids=cores).results
    xc_all = np.stack([np.asarray(r_b[c]["xc"]) for c in cores])
    xcg = np.zeros((2, 16, 128, T), dtype=ml_dtypes.bfloat16)
    for c in cores:
        b, sl = c // 4, (c % 4)
        xcg[b, :, :, sl*SH:(sl+1)*SH] = xc_all[c]
    ssm_all = np.concatenate([np.asarray(r_b[c]["ssm"]) for c in cores])
    Bp_g = ssm_all[:, 0:16].reshape(2, T, 16)
    Cp_g = ssm_all[:, 16:32].reshape(2, T, 16)
    ssml_g = ssm_all[:, 32].reshape(2, T)
    tri = np.triu(np.ones((32, 32), np.float32))
    tri128 = np.zeros((128, 128), np.float32)
    for k in range(4):
        tri128[k*32:(k+1)*32, k*32:(k+1)*32] = tri
    sel4 = np.zeros((128, 4), np.float32)
    for r in range(4):
        sel4[31+32*r, r] = 1.0
    ind4 = np.zeros((4, 128), np.float32)
    for r in range(4):
        ind4[r, r*32:(r+1)*32] = 1.0
    ims = []
    units_by_core = []
    for c in cores:
        units = [(u // 16, u % 16) for u in range(4*c, 4*c+4)]
        units_by_core.append(units)
        xcu = np.stack([np.ascontiguousarray(xcg[b, ch].T) for b, ch in units])
        ims.append({
            "xcu": xcu,
            "ssml": np.stack([ssml_g[b] for b, ch in units]),
            "dtw": np.stack([dtwF[0, ch*128:(ch+1)*128].reshape(1, 128)
                             for b, ch in units]),
            "dtb": np.stack([dtbF[ch*128:(ch+1)*128].reshape(1, 128)
                             for b, ch in units]),
            "alog": np.stack([alogF[ch*128:(ch+1)*128] for b, ch in units]),
            "dpar": np.stack([dparF[ch*128:(ch+1)*128].reshape(1, 128)
                              for b, ch in units]),
            "Bp": np.stack([Bp_g[b] for b, ch in units]),
            "Cp": np.stack([Cp_g[b] for b, ch in units]),
            "triu32": tri128, "sel4": sel4, "ind4": ind4})
    r_s = run_bass_kernel_spmd(build_L1s(), ims, core_ids=cores).results
    y_g = np.zeros((2, T, 2048), dtype=np.float32)
    for c in cores:
        ys = np.asarray(r_s[c]["y"]).astype(np.float32)
        for u, (b, ch) in enumerate(units_by_core[c]):
            y_g[b, :, ch*128:(ch+1)*128] = ys[u]
    ims = []
    for c in cores:
        b = c // 4
        kfull = np.concatenate([np.asarray(r_a2[p]["k"])
                                for p in range(b*4, b*4+4)], axis=1)
        vfull = np.concatenate([np.asarray(r_a2[p]["v"])
                                for p in range(b*4, b*4+4)], axis=0)
        w_rtT = np.ascontiguousarray(np.asarray(r_a[c]["w_rt"]).T)
        ims.append({"q": np.asarray(r_a2[c]["q"]), "kf": kfull, "vf": vfull,
                    "w_rtT": w_rtT, "oproj": opF})
    r_fb = run_bass_kernel_spmd(build_L1fb(), ims, core_ids=cores).results
    ims = []
    for c in cores:
        b, sl = c // 4, c % 4
        y_cm = np.zeros((16, 128, SH), dtype=ml_dtypes.bfloat16)
        for ch in range(16):
            y_cm[ch] = y_g[b, sl*SH:(sl+1)*SH, ch*128:(ch+1)*128].T
        ims.append({"y_cm": y_cm, "z_cm": np.asarray(r_a[c]["z"]), "opw": opw,
                    "g1": np.asarray(r_fb[c]["g1"]),
                    "x1": np.ascontiguousarray(x1[c*SH:(c+1)*SH])})
    r_d = run_bass_kernel_spmd(build_L1d(), ims, core_ids=cores).results
    x2 = np.concatenate([np.asarray(r_d[c]["x2"]) for c in cores])

    g2 = lambda n: f("L2_attn_" + n)
    wnorm, convw, convb = g2("w_norm_in"), g2("conv_w"), g2("conv_b")
    wq, wk, wv, wout = g2("wq"), g2("wk"), g2("wv"), g2("w_out")
    wsp, wau, bau = g2("w_spike"), g2("w_au"), g2("b_au")
    wad, bad = g2("w_ad"), g2("b_ad")
    aspF, wbet, bbet, bspF = (g2("alpha_spike"), g2("w_beta"), g2("b_beta"),
                              g2("beta_spike"))
    whnF, wu1, bu1, wu2, bu2 = (g2("w_hn"), g2("w_u1"), g2("b_u1"),
                                g2("w_u2"), g2("b_u2"))
    wffn, ff1, ff3, ff2 = g2("w_ffn"), g2("w_ff1"), g2("w_ff3"), g2("w_ff2")
    Win, bin_ = f("L2_ffn_W_in"), f("L2_ffn_b_in")
    WoutF, bout = f("L2_ffn_W_out"), f("L2_ffn_b_out")

    ims = []
    for c in cores:
        if c % 4 == 0:
            xh = np.zeros((3, 1024), np.float32)
        else:
            xh = x2[c * SH - 3: c * SH]
        ims.append({"x": np.ascontiguousarray(x2[c*SH:(c+1)*SH]),
                    "xhal": np.ascontiguousarray(xh),
                    "wnorm": wnorm.reshape(1, 1024),
                    "cw4": np.ascontiguousarray(convw.reshape(8, 128, 4)),
                    "cb4": np.ascontiguousarray(convb.reshape(8, 128, 1)),
                    "wq": wq, "wk": wk, "wv": wv, "wsp": wsp, "wau": wau,
                    "bau": bau.reshape(1, 64), "wad": wad,
                    "bad": bad.reshape(1, 512), "wbet": wbet,
                    "bbet": bbet.reshape(1, 8), "wu1": wu1,
                    "bu1": bu1.reshape(1, 64), "wu2": wu2,
                    "bu2": bu2.reshape(1, 1024), "ident": ident})
    r_a = run_bass_kernel_spmd(build_L2a(), ims, core_ids=cores).results
    def gather(name, dim):
        return np.concatenate([np.asarray(r_a[c][name]) for c in cores]
                              ).reshape(2, T, dim)
    qg, kg = gather("qo", 512), gather("ko", 512)
    vg, sdg = gather("vo", 1024), gather("sdo", 512)
    abg, bbg = gather("abo", 512), gather("bbo", 8)
    maskL = np.tril(np.ones((32, 32), np.float32), -1)
    maskU = np.triu(np.ones((32, 32), np.float32), 1)
    maskUI = np.triu(np.ones((32, 32), np.float32), 0)
    pones = np.zeros((128, 2), np.float32)
    pones[0:64, 0] = 1; pones[64:128, 1] = 1
    pind = np.zeros((2, 128), np.float32)
    pind[0, 0:64] = 1; pind[1, 64:128] = 1
    ims = []
    for c in cores:
        pairs = [(2*c) // 8, (2*c) % 8], [(2*c+1) // 8, (2*c+1) % 8]
        def sl128(a, dimh):
            out = np.zeros((128, T), np.float32)
            for p, (b, h) in enumerate(pairs):
                out[p*64:(p+1)*64] = a[b, :, h*dimh:(h+1)*dimh].T[:64]
            return out
        qsl, ksl = sl128(qg, 64), sl128(kg, 64)
        sdsl, absl = sl128(sdg, 64), sl128(abg, 64)
        bbr = np.zeros((128, T), np.float32)
        vsl = np.zeros((2, T, 128), np.float32)
        asp = np.zeros((128, 1), np.float32)
        bsp = np.zeros((128, 1), np.float32)
        whn = np.zeros((2, 1, 128), np.float32)
        for p, (b, h) in enumerate(pairs):
            bbr[p*64:(p+1)*64] = bbg[b, :, h][None, :]
            vsl[p] = vg[b, :, h*128:(h+1)*128]
            asp[p*64:(p+1)*64, 0] = aspF[0, h]
            bsp[p*64:(p+1)*64, 0] = bspF[0, h]
            whn[p, 0] = whnF[h]
        ims.append({"qsl": qsl, "ksl": ksl, "sdsl": sdsl, "absl": absl,
                    "bbrep": bbr, "vsl": vsl, "asp": asp, "bsp": bsp,
                    "whn": whn, "pones": pones, "pind": pind, "maskL": maskL,
                    "maskU": maskU, "maskUI": maskUI, "ident": ident})
    r_b = run_bass_kernel_spmd(build_L2b(), ims, core_ids=cores).results
    yg = np.zeros((2, T, 8, 128), np.float32)
    for c in cores:
        yv = np.asarray(r_b[c]["y"])
        for p in range(2):
            gpi = 2 * c + p
            b, h = gpi // 8, gpi % 8
            yg[b, :, h, :] = yv[p]
    y_t = yg.reshape(2 * T, 1024)
    ims = []
    for c in cores:
        ims.append({"y_t": np.ascontiguousarray(y_t[c*SH:(c+1)*SH]),
                    "gate": np.asarray(r_a[c]["gate"]),
                    "xn_t": np.asarray(r_a[c]["xn_t"]),
                    "wout": wout, "wffn": wffn.reshape(1, 1024),
                    "ff1": ff1, "ff3": ff3, "ff2": ff2, "ident": ident})
    r_c = run_bass_kernel_spmd(build_L2c(), ims, core_ids=cores).results
    xnmax_all = np.concatenate([np.asarray(r_a[c]["xnmax"]).ravel()
                                for c in cores]).reshape(1, 1024).astype(np.float32)
    ims = []
    for c in cores:
        ims.append({"xnfm": np.asarray(r_a[c]["xnfm"]), "xnmax_all": xnmax_all,
                    "Win": Win,
                    "binf": np.ascontiguousarray(bin_.reshape(32, 128).T),
                    "Wout": WoutF, "bor": bout.reshape(1, 1024),
                    "s2": np.asarray(r_c[c]["s2"]),
                    "x": np.ascontiguousarray(x2[c*SH:(c+1)*SH]),
                    "ident": ident})
    r_f = run_bass_kernel_spmd(build_L2f(), ims, core_ids=cores).results
    x3 = np.concatenate([np.asarray(r_f[c]["x3"]) for c in cores])

    Wq, bq = f("L3_attn_Wq"), f("L3_attn_bq")
    Wk, bk = f("L3_attn_Wk"), f("L3_attn_bk")
    Wv, bv = f("L3_attn_Wv"), f("L3_attn_bv")
    Wo, bo = f("L3_attn_Wo"), f("L3_attn_bo")
    proj = f("L3_attn_proj")
    lng, lnb = f("L3_ffn_ln_gamma"), f("L3_ffn_ln_beta")
    abw, abb = f("L3_ffn_abel_w"), f("L3_ffn_abel_b")
    mgw, mgb = f("L3_ffn_moe_gate_w"), f("L3_ffn_moe_gate_b")
    ew1, eb1 = f("L3_ffn_ew1"), f("L3_ffn_eb1")
    ew2, eb2 = f("L3_ffn_ew2"), f("L3_ffn_eb2")
    gens = f("L3_ffn_gens")
    gcw, gcb = f("L3_ffn_gc_w"), f("L3_ffn_gc_b")

    eye = np.eye(1024, dtype=np.float32)
    r_r = run_bass_kernel_spmd(build_L3r(), [
        {"gens": gens, "islab": np.ascontiguousarray(eye[:, c*128:(c+1)*128])}
        for c in cores], core_ids=cores).results
    R3s = np.stack([np.concatenate([np.asarray(r_r[c]["Rslab"])[i]
                                    for c in cores], axis=1)
                    for i in range(3)]).astype(np.float32)
    wqkvT = np.ascontiguousarray(np.concatenate([Wq.T, Wk.T, Wv.T], axis=1))
    bqkv = np.concatenate([bq, bk, bv]).reshape(1, 3072)
    projT = np.ascontiguousarray(proj.T)
    r_a = run_bass_kernel_spmd(build_L3a(), [
        {"x": np.ascontiguousarray(x3[c*SH:(c+1)*SH]), "wqkvT": wqkvT,
         "bqkv": bqkv, "projT": projT, "ident": ident} for c in cores],
        core_ids=cores).results
    kmax_all = np.array([[np.asarray(r_a[c]["kmax"])[0, 0] for c in cores]],
                        np.float32)
    q_full = [np.concatenate([np.asarray(r_a[4*b+j]["q_t"]) for j in range(4)])
              for b in range(2)]
    k_full = [np.concatenate([np.asarray(r_a[4*b+j]["k_t"]) for j in range(4)])
              for b in range(2)]
    v_full = [np.concatenate([np.asarray(r_a[4*b+j]["v_t"]) for j in range(4)])
              for b in range(2)]
    ims = []
    for c in cores:
        qp = np.empty((4, T, 64), np.float32)
        kp = np.empty((4, T, 64), np.float32)
        vp = np.empty((4, T, 64), np.float32)
        for p in range(4):
            gpi = 4 * c + p
            b, h = gpi // 16, gpi % 16
            qp[p] = q_full[b][:, h*64:(h+1)*64]
            kp[p] = k_full[b][:, h*64:(h+1)*64]
            vp[p] = v_full[b][:, h*64:(h+1)*64]
        ims.append({"qp": qp, "kp": kp, "vp": vp, "kmax_all": kmax_all,
                    "projT": projT, "ident": ident})
    r_b = run_bass_kernel_spmd(build_L3b(), ims, core_ids=cores).results
    o_full = [np.empty((T, 1024), np.float32) for _ in range(2)]
    for b in range(2):
        for h in range(16):
            gpi = 16 * b + h
            c, p = gpi // 4, gpi % 4
            o_full[b][:, h*64:(h+1)*64] = np.asarray(r_b[c]["o"])[p]
    o_all = np.concatenate(o_full)
    xn4_all = np.concatenate([np.asarray(r_a[c]["xn4"]) for c in cores])
    ims = []
    for c in cores:
        sl = slice(c*SH, (c+1)*SH)
        ims.append({"o_as": np.ascontiguousarray(o_all[sl]),
                    "xn4": np.ascontiguousarray(xn4_all[sl]),
                    "x3": np.ascontiguousarray(x3[sl]),
                    "woT": np.ascontiguousarray(Wo.T),
                    "bo": bo.reshape(1, 1024),
                    "lng": lng.reshape(1, 1024), "lnb": lnb.reshape(1, 1024),
                    "abw": abw, "abb": abb.reshape(1, 1024),
                    "mgw": mgw, "mgb": mgb.reshape(1, 8),
                    "ew1": ew1, "eb1": eb1, "ew2": ew2, "eb2": eb2,
                    "R3s": R3s, "gcw": gcw.reshape(1, 2048),
                    "gcb": gcb.reshape(1, 1), "ident": ident})
    r_c = run_bass_kernel_spmd(build_L3c(), ims, core_ids=cores).results
    x4 = np.concatenate([np.asarray(r_c[c]["x4"]) for c in cores])
    return x4.reshape(2, T, D).astype(np.float32)
